# version 12
# baseline (speedup 1.0000x reference)
import contextlib
import numpy as np

import concourse.bass as bass
import concourse.mybir as mybir
from concourse.bass_utils import run_bass_kernel_spmd

F32 = mybir.dt.float32
BF16 = mybir.dt.bfloat16
I32 = mybir.dt.int32
AF = mybir.ActivationFunctionType
ALU = mybir.AluOpType
AX = mybir.AxisListType

P = 128
T = 4096
D = 1024
HID = 4096
E = 8
NH = 16
HD = 64
TT = T // P
DT = D // P
HT = HID // P
CH = 8
CT = T // CH
CTT = CT // P
NCH = 16
EPS = 1e-5

_CHAINED = {
    "tensor_copy", "tensor_tensor", "tensor_add", "tensor_sub", "tensor_mul",
    "tensor_max", "tensor_scalar", "tensor_single_scalar", "tensor_scalar_add",
    "tensor_scalar_sub", "tensor_scalar_mul", "tensor_scalar_max", "tensor_relu",
    "tensor_reduce", "reduce_max", "reduce_sum", "scalar_tensor_tensor",
    "reciprocal", "memset", "memzero", "select", "copy_predicated", "iota",
    "affine_select", "activation", "square", "sqrt", "mul", "copy", "add", "sign",
}


_RISKY = {"tensor_reduce", "reduce_max", "reduce_sum", "reciprocal",
          "bn_stats", "bn_aggr"}
import os as _os
STRICT_CHAIN = _os.environ.get("STRICT_CHAIN", "vch")


class Chained:

    def __init__(self, eng, sem, name=""):
        self.eng = eng
        self.sem = sem
        self.name = name
        self.strict = ("all" in STRICT_CHAIN) or any(
            t and t in name for t in STRICT_CHAIN.split(","))
        self.n = 0
        self.pend = None

    def drain(self):
        if self.pend:
            self.eng.wait_ge(*self.pend)
            self.pend = None

    def __getattr__(self, name):
        eng = self.eng
        f = getattr(eng, name)
        if name not in _CHAINED:
            return f
        base_risky = self.strict or name in _RISKY

        def wrapped(*a, inc=None, **kw):
            risky = base_risky or kw.get("accum_out") is not None
            if self.pend:
                eng.wait_ge(*self.pend)
                self.pend = None
            inst = f(*a, **kw)
            if inc is not None:
                sem, val = inc
                inst.then_inc(sem, 1)
                if risky:
                    self.pend = (sem, val)
            elif risky:
                self.n += 1
                inst.then_inc(self.sem, 1)
                self.pend = (self.sem, self.n)
            return inst

        return wrapped


class Prog:
    def __init__(self, nc, stack):
        self.nc = nc
        self.stack = stack
        self.pstack = None
        self.sems = {}
        self.phase_sems = []

    def sem(self, name):
        if name not in self.sems:
            st = self.pstack if self.pstack is not None else self.stack
            h = st.enter_context(self.nc.semaphore(name))
            self.sems[name] = h
            if self.pstack is not None:
                self.phase_sems.append(h)
        return self.sems[name]

    def gsem(self, name):
        if name not in self.sems:
            h = self.stack.enter_context(self.nc.semaphore(name))
            self.sems[name] = h
        return self.sems[name]

    def clear_block(self, tag):
        sems = list(self.phase_sems)
        with self.nc.Block(f"{tag}_clr") as b:
            @b.sync
            def _(s):
                for h in sems:
                    s.sem_clear(h)

    def chain(self, eng, name):
        return Chained(eng, self.sem(name), name)

    def sbuf_g(self, name, shape, dtype):
        return self.stack.enter_context(self.nc.sbuf_tensor(name, shape, dtype))

    def sbuf(self, name, shape, dtype):
        return self.pstack.enter_context(self.nc.sbuf_tensor(name, shape, dtype))

    def psum(self, name, shape, dtype=F32):
        return self.pstack.enter_context(self.nc.psum_tensor(name, shape, dtype))

    @contextlib.contextmanager
    def phase(self):
        with contextlib.ExitStack() as ps:
            self.pstack = ps
            self.phase_sems = []
            yield
            self.pstack = None
            for h in self.phase_sems:
                self.sems = {k: v for k, v in self.sems.items() if v is not h}
            self.phase_sems = []


def emit_setup(p, io):
    nc = p.nc
    p.ident_bf = p.sbuf_g("ident_bf", [P, P], BF16)
    p.ident_f32 = p.sbuf_g("ident_f32", [P, P], F32)
    p.ones3 = p.sbuf_g("ones3", [3, P], BF16)
    p.ones_bf = p.sbuf_g("ones_bf", [P, P], BF16)
    p.mneg1 = p.sbuf_g("mneg1_bf", [P, P], BF16)
    p.mnegT = p.sbuf_g("mnegT_bf", [P, P], BF16)
    p.striu_bf = p.sbuf_g("striu_bf", [P, P], BF16)
    p.wg_sb = p.sbuf_g("wg_sb", [P, DT * E], BF16)
    p.bgB = p.sbuf_g("bgB", [P, E], F32)
    p.slots = p.sbuf_g("slots_i32", [P, TT], mybir.dt.int32)
    p.halfw = p.sbuf_g("halfw_sb", [P, TT], F32)
    p.eps_col = p.sbuf_g("eps_col", [P, 1], F32)
    p.nc20 = p.sbuf_g("nc20_col", [P, 1], F32)
    p.MSK32 = p.sbuf_g("msk32", [P, TT * E], F32)
    p.RK32 = p.sbuf_g("rk32", [P, TT], F32)
    p.IDXW = p.sbuf_g("idxw_i32", [P, NCH], I32)
    p.IDXB1 = p.sbuf_g("idxb1_i32", [P, NCH], I32)
    p.IDXE = p.sbuf_g("idxe_i32", [P, NCH], I32)
    p.pcolf = p.sbuf_g("pcolf", [P, 1], F32)
    p.CCrowF = p.sbuf_g("ccrowf", [P, NCH], F32)
    p.TK32 = p.sbuf_g("tk32_i32", [P, TT], I32)

    sd = p.sem("setup_dma")
    sdg = p.gsem("setup_dma_g")
    with p.phase():
        scratch = p.sbuf("setup_scratch", [P, P], F32)
        with nc.Block("setup") as b:
            @b.sync
            def _(s):
                s.dma_start(io.xwork[:, :], io.x_in[:, :]).then_inc(sd, 16)
                s.dma_start(p.bgB[:, :],
                            bass.AP(io.bg.tensor, 0, [[0, P], [1, E]])).then_inc(sd, 16)
                s.wait_ge(sd, 32)
                s.wait_ge(sdg, 16)

            @b.gpsimd
            def _(g0):
                g0.dma_start(p.wg_sb[:, :].rearrange("p (dt e) -> p dt e", dt=DT),
                             io.wg.rearrange("(dt p) e -> p dt e", p=P)).then_inc(sdg, 16)
                g = p.chain(g0, "setup_gch")
                g.memset(p.ones3[:, :], 1.0)
                g.memset(p.eps_col[:, :], EPS)
                g.memset(p.nc20[:, :], -20.0)
                g.memset(p.ones_bf[:, :], 1.0)
                g.iota(p.pcolf[:, :], [[1, 1]], channel_multiplier=1,
                       allow_small_or_imprecise_dtypes=True)
                g.iota(p.CCrowF[:, :], [[1, NCH]], channel_multiplier=0,
                       allow_small_or_imprecise_dtypes=True)
                g.iota(p.TK32[:, :], [[P, TT]], channel_multiplier=1,
                       allow_small_or_imprecise_dtypes=True)
                g.memset(scratch[:, :], 1.0)
                g.affine_select(p.ident_bf[:, :], scratch[:, :], [[1, P]], ALU.is_equal,
                                0.0, channel_multiplier=-1)
                g.affine_select(p.ident_f32[:, :], scratch[:, :], [[1, P]], ALU.is_equal,
                                0.0, channel_multiplier=-1)
                g.affine_select(p.striu_bf[:, :], p.ones_bf[:, :], [[1, P]], ALU.is_ge,
                                0.0, base=-1, channel_multiplier=-1)
                g.memset(scratch[:, :], -10000.0)
                g.affine_select(p.mneg1[:, :], scratch[:, :], [[1, P]], ALU.is_ge,
                                0.0, base=-1, channel_multiplier=-1)
                g.affine_select(p.mnegT[:, :], scratch[:, :], [[-1, P]], ALU.is_ge,
                                0.0, base=-1, channel_multiplier=1)
                g.drain()
                g0.wait_ge(sd, 32)
                g0.wait_ge(sdg, 16)


def emit_ln(p, io, idx, src, with_gate, precast=False):
    nc = p.nc
    tag = f"ln{idx}"
    p.phase_ctx = p.phase()
    p.phase_ctx.__enter__()
    sx = [p.sem(f"{tag}_sx{i}") for i in range(2)]
    ssq = p.sem(f"{tag}_ssq")
    svar = p.sem(f"{tag}_svar")
    ssd = p.sem(f"{tag}_ssd")
    srec = p.sem(f"{tag}_srec")
    sln = p.sem(f"{tag}_sln")
    smm8 = p.sem(f"{tag}_smm8")
    scp8 = p.sem(f"{tag}_scp8")
    slt = [p.sem(f"{tag}_slt{i}") for i in range(2)]
    if with_gate:
        sgm = p.sem(f"{tag}_sgm")
        sgv = p.sem(f"{tag}_sgv")
        sge = p.sem(f"{tag}_sge")
        sgd = p.sem(f"{tag}_sgd")
        smk = p.sem(f"{tag}_smk")
        srk = p.sem(f"{tag}_srk")
        ssl = p.sem(f"{tag}_ssl")
        ssl2 = p.sem(f"{tag}_ssl2")
        sld8 = p.sem(f"{tag}_sld8")
        svi = p.sem(f"{tag}_svi")
        stin = p.sem(f"{tag}_stin")
        slg = [p.gsem(f"{tag}_slg{i}") for i in range(2)]
    if precast:
        swc = [p.gsem(f"{tag}_swc{i}") for i in range(2)]
        swcs = [p.sem(f"{tag}_swcs{i}") for i in range(2)]
    p.sem(f"{tag}_sch")
    p.sem(f"{tag}_vch")
    p.clear_block(tag)
    if True:
        xb = [p.sbuf(f"{tag}_x{i}", [P, D], F32) for i in range(2)]
        sq = [p.sbuf(f"{tag}_sq{i}", [P, D], F32) for i in range(2)]
        lnb = [p.sbuf(f"{tag}_ln{i}", [P, D], BF16) for i in range(2)]
        stag = [p.sbuf(f"{tag}_st{i}", [P, D], BF16) for i in range(2)]
        cols = {n: p.sbuf(f"{tag}_{n}", [P, 1], F32)
                for n in ["s1", "s2", "mean", "var", "sd", "rstd", "negmb", "m8", "nm8",
                          "S", "w", "t8"]}
        g8 = p.sbuf(f"{tag}_g8", [P, E], F32)
        sgn8 = p.sbuf(f"{tag}_sgn8", [P, E], F32)
        p8 = p.sbuf(f"{tag}_p8", [P, E], F32)
        gt8 = p.sbuf(f"{tag}_gt8", [P, E], F32)
        trp = [p.psum(f"{tag}_trp{i}", [P, P], BF16) for i in range(2)]
        lgp = [p.psum(f"{tag}_lgp{i}", [P, E], F32) for i in range(2)] if with_gate else None
        if with_gate:
            maskbf = p.sbuf(f"{tag}_maskbf", [P, TT * E], BF16)
            rkp = [p.psum(f"{tag}_rkp{i}", [P, E], F32) for i in range(2)]
            slotf = p.sbuf(f"{tag}_slotf", [P, 1], F32)
            r8 = p.sbuf(f"{tag}_r8", [P, E], F32)
            l8 = p.sbuf(f"{tag}_l8", [P, E], F32)
            nch8 = p.sbuf(f"{tag}_nch8", [P, E], F32)
            cpa = p.sbuf(f"{tag}_cpa", [P, E], F32)
            cpb = p.sbuf(f"{tag}_cpb", [P, E], F32)
            padoff8 = p.sbuf(f"{tag}_padoff8", [P, E], F32)
            ecf = p.sbuf(f"{tag}_ecf", [P, NCH], F32)
            ect = p.sbuf(f"{tag}_ect", [P, NCH], F32)
            mtf = p.sbuf(f"{tag}_mtf", [P, NCH * CT // P], F32)
            mti = p.sbuf(f"{tag}_mti", [P, NCH * CT // P], I32)
        if precast:
            wcs = [p.sbuf(f"{tag}_wc{i}", [P, 8192], BF16) for i in range(2)]
            witems = [("w1", e, dt) for e in range(E) for dt in range(DT)] + \
                     [("w2", e, g4) for e in range(E) for g4 in range(4)]

        with nc.Block(tag) as b:
            if with_gate:
                @b.gpsimd
                def _(g):
                    g.wait_ge(svi, 1)
                    g.dma_start(bass.AP(io.tokof.tensor, 0,
                                        [[NCH * CT // P, P], [1, NCH * CT // P]]),
                                mti[:, :]).then_inc(stin, 16)
                    if precast:
                        for i, (kind, e, j) in enumerate(witems):
                            if i >= 2:
                                g.wait_ge(swcs[i % 2], 16 * (i // 2))
                            if kind == "w1":
                                g.dma_start(
                                    wcs[i % 2][:, 0:HID],
                                    io.w1[e][j * P:(j + 1) * P, :],
                                ).then_inc(swc[i % 2], 16)
                            else:
                                g.dma_start(
                                    wcs[i % 2][:, :].rearrange(
                                        "p (g d) -> p g d", g=8),
                                    io.w2[e][j * 1024:(j + 1) * 1024, :].rearrange(
                                        "(g p) d -> p g d", p=P),
                                ).then_inc(swc[i % 2], 16)
                    g.wait_ge(stin, 16)
                    for tt in range(TT):
                        g.wait_ge(ssl2, tt + 1)
                        g.indirect_dma_start(
                            io.tokof,
                            bass.IndirectOffsetOnAxis(p.slots[:, tt:tt + 1], 0),
                            p.TK32[:, tt:tt + 1], None).then_inc(slg[tt % 2], 16)
                    g.wait_ge(slg[0], 16 * (TT // 2))
                    g.wait_ge(slg[1], 16 * (TT // 2))

            @b.sync
            def _(s):
                if precast:
                    wstore = [0]

                    def emit_wstore():
                        i = wstore[0]
                        if i >= len(witems):
                            return
                        wstore[0] += 1
                        kind, e, j = witems[i]
                        s.wait_ge(swc[i % 2], 16 * (i // 2 + 1))
                        if kind == "w1":
                            s.dma_start(
                                io.W1bf[e * P:(e + 1) * P,
                                        j * HID:(j + 1) * HID],
                                wcs[i % 2][:, 0:HID]).then_inc(swcs[i % 2], 16)
                        else:
                            s.dma_start(
                                io.W2bf[e * P:(e + 1) * P,
                                        j * 8192:(j + 1) * 8192],
                                wcs[i % 2][:, :]).then_inc(swcs[i % 2], 16)

                for tt in range(TT):
                    if tt >= 2:
                        s.wait_ge(sln, tt - 1)
                        s.wait_ge(svar, tt - 1)
                    s.dma_start(xb[tt % 2][:, :],
                                src[tt * P:(tt + 1) * P, :]).then_inc(sx[tt % 2], 16)
                    if with_gate:
                        s.wait_ge(sln, tt + 1)
                        s.dma_start(io.lnR[tt * P:(tt + 1) * P, :],
                                    lnb[tt % 2][:, :]).then_inc(slt[tt % 2], 16)
                    else:
                        s.wait_ge(scp8, (tt + 1) * DT)
                        s.dma_start(
                            io.lnT.rearrange("(dt p) t -> p dt t", p=P)[:, :, tt * P:(tt + 1) * P],
                            stag[tt % 2][:, :].rearrange("p (dt c) -> p dt c", dt=DT),
                        ).then_inc(slt[tt % 2], 16)
                    if precast:
                        for _ in range(3):
                            emit_wstore()
                if precast:
                    while wstore[0] < len(witems):
                        emit_wstore()
                    s.wait_ge(swcs[0], 16 * (len(witems) // 2))
                    s.wait_ge(swcs[1], 16 * (len(witems) // 2))
                s.wait_ge(slt[0], 16 * (TT // 2))
                s.wait_ge(slt[1], 16 * (TT // 2))

            @b.scalar
            def _(s0):
                s = p.chain(s0, f"{tag}_sch")
                for tt in range(TT):
                    s0.wait_ge(sx[tt % 2], 16 * (tt // 2 + 1))
                    if tt >= 2:
                        s0.wait_ge(svar, tt - 1)
                    s.square(sq[tt % 2][:, :], xb[tt % 2][:, :], inc=(ssq, tt + 1))
                    s0.wait_ge(svar, tt + 1)
                    if tt >= 1:
                        s0.wait_ge(srec, tt)
                    s.activation(cols["sd"][:, :], cols["var"][:, :], AF.Sqrt,
                                 bias=p.eps_col[:, :], inc=(ssd, tt + 1))
                    s0.wait_ge(srec, tt + 1)
                    if tt >= 2:
                        s0.wait_ge(scp8, (tt - 1) * DT)
                        if with_gate:
                            s0.wait_ge(slt[tt % 2], 16 * (tt // 2))
                    s.activation(lnb[tt % 2][:, :], xb[tt % 2][:, :], AF.Identity,
                                 bias=cols["negmb"][:, :],
                                 scale=cols["rstd"][:, :], inc=(sln, tt + 1))
                    if with_gate:
                        s0.wait_ge(sgv, tt + 1)
                        if tt >= 1:
                            s0.wait_ge(sgd, tt)
                        s.activation(p8[:, :], g8[:, :], AF.Exp, bias=cols["nm8"][:, :],
                                     accum_out=cols["S"][:, :])
                        s.activation(sgn8[:, :], g8[:, :], AF.Sign,
                                     bias=cols["nm8"][:, :], inc=(sge, tt + 1))

            @b.vector
            def _(v0):
                v = p.chain(v0, f"{tag}_vch")
                if with_gate:
                    v.memset(mtf[:, :], float(T - 1))
                    v.tensor_copy(mti[:, :], mtf[:, :], inc=(svi, 1))
                for tt in range(TT):
                    v0.wait_ge(sx[tt % 2], 16 * (tt // 2 + 1))
                    v.reduce_sum(cols["s1"][:, :], xb[tt % 2][:, :], AX.X)
                    v0.wait_ge(ssq, tt + 1)
                    v.reduce_sum(cols["s2"][:, :], sq[tt % 2][:, :], AX.X)
                    v.tensor_scalar(cols["mean"][:, :], cols["s1"][:, :], 1.0 / D, None,
                                    ALU.mult)
                    v.tensor_scalar(cols["var"][:, :], cols["s2"][:, :], 1.0 / D, None,
                                    ALU.mult)
                    v.tensor_tensor(cols["s1"][:, :], cols["mean"][:, :],
                                    cols["mean"][:, :], ALU.mult)
                    v.tensor_tensor(cols["var"][:, :], cols["var"][:, :], cols["s1"][:, :],
                                    ALU.subtract, inc=(svar, tt + 1))
                    v0.wait_ge(ssd, tt + 1)
                    if tt >= 1:
                        v0.wait_ge(sln, tt)
                    v.reciprocal(cols["rstd"][:, :], cols["sd"][:, :])
                    v.tensor_scalar(cols["negmb"][:, :], cols["mean"][:, :],
                                    cols["rstd"][:, :], -1.0, ALU.mult,
                                    ALU.mult, inc=(srec, tt + 1))
                    if tt >= 2:
                        if with_gate:
                            v0.wait_ge(sgm, tt - 1)
                        else:
                            v0.wait_ge(slt[tt % 2], 16 * (tt // 2))
                    for dt in range(DT):
                        n = tt * DT + dt
                        v0.wait_ge(smm8, n + 1)
                        v.tensor_copy(stag[tt % 2][:, dt * P:(dt + 1) * P],
                                      trp[dt % 2][:, :], inc=(scp8, n + 1))
                    if with_gate:
                        v0.wait_ge(sgm, tt + 1)
                        if tt >= 1:
                            v0.wait_ge(sge, tt)
                        v.tensor_tensor(g8[:, :], lgp[tt % 2][:, :], p.bgB[:, :], ALU.add)
                        v.reduce_max(cols["m8"][:, :], g8[:, :], AX.X)
                        v.reduce_max(cols["nm8"][:, :], g8[:, :], AX.X,
                                     negate=True, inc=(sgv, tt + 1))
                        v0.wait_ge(sge, tt + 1)
                        v.reciprocal(cols["w"][:, :], cols["S"][:, :])
                        v.tensor_scalar(cols["t8"][:, :], cols["w"][:, :], 0.5, None,
                                        ALU.mult)
                        v.tensor_copy(p.halfw[:, tt:tt + 1], cols["t8"][:, :])
                        v.tensor_scalar(gt8[:, :], sgn8[:, :], 1.0, None, ALU.add)
                        v.tensor_copy(maskbf[:, tt * E:(tt + 1) * E], gt8[:, :],
                                      inc=(smk, tt + 1))
                        v.tensor_copy(p.MSK32[:, tt * E:(tt + 1) * E], gt8[:, :])
                        v.tensor_scalar(gt8[:, :], gt8[:, :], 1.0, None,
                                        ALU.mult, inc=(sgd, tt + 1))
                        v0.wait_ge(srk, tt + 1)
                        v.tensor_tensor(r8[:, :], rkp[tt % 2][:, :], gt8[:, :], ALU.mult)
                        v.reduce_sum(slotf[:, :], r8[:, :], AX.X)
                        v.tensor_copy(p.RK32[:, tt:tt + 1], slotf[:, :],
                                      inc=(ssl, tt + 1))
                if with_gate:
                    v0.wait_ge(sld8, 1)
                    v.tensor_copy(l8[:, :], lgp[0][:, :])
                    v.memset(nch8[:, :], 0.0)
                    for kk in range(NCH):
                        v.tensor_scalar(cpa[:, :], l8[:, :], -float(kk * CT),
                                        None, ALU.add)
                        v.tensor_scalar(cpa[:, :], cpa[:, :], 0.0, 1.0,
                                        ALU.max, ALU.min)
                        v.tensor_tensor(nch8[:, :], nch8[:, :], cpa[:, :], ALU.add)
                    v.tensor_copy(cpa[:, :], nch8[:, :])
                    v.tensor_copy(cpb[:, :], cpa[:, :])
                    v.tensor_tensor(cpb[:, 1:E], cpa[:, 1:E],
                                    cpa[:, 0:E - 1], ALU.add)
                    v.tensor_copy(cpa[:, :], cpb[:, :])
                    v.tensor_tensor(cpa[:, 2:E], cpb[:, 2:E],
                                    cpb[:, 0:E - 2], ALU.add)
                    v.tensor_copy(cpb[:, :], cpa[:, :])
                    v.tensor_tensor(cpb[:, 4:E], cpa[:, 4:E],
                                    cpa[:, 0:E - 4], ALU.add)
                    v.tensor_tensor(cpb[:, :], cpb[:, :], nch8[:, :], ALU.subtract)
                    v.tensor_scalar(padoff8[:, :], cpb[:, :], float(CT), None,
                                    ALU.mult)
                    v.memset(ecf[:, :], -1.0)
                    for e in range(E):
                        v.tensor_scalar(ect[:, :], p.CCrowF[:, :],
                                        cpb[:, e:e + 1], 1.0,
                                        ALU.subtract, ALU.add)
                        v.tensor_scalar(ect[:, :], ect[:, :], 0.0, 1.0,
                                        ALU.max, ALU.min)
                        v.tensor_tensor(ecf[:, :], ecf[:, :], ect[:, :], ALU.add)
                    v.tensor_copy(p.IDXE[:, :], ecf[:, :])
                    v.tensor_scalar(ect[:, :], ecf[:, :], 32.0, None, ALU.mult)
                    v.tensor_scalar(ect[:, :], ect[:, :], p.pcolf[:, 0:1],
                                    None, ALU.add)
                    v.tensor_copy(p.IDXB1[:, :], ect[:, :])
                    v.tensor_scalar(ect[:, :], ecf[:, :], 128.0, None, ALU.mult)
                    v.tensor_scalar(ect[:, :], ect[:, :], p.pcolf[:, 0:1],
                                    None, ALU.add)
                    v.tensor_copy(p.IDXW[:, :], ect[:, :])
                    for tt in range(TT):
                        v.tensor_tensor(r8[:, :], p.MSK32[:, tt * E:(tt + 1) * E],
                                        padoff8[:, :], ALU.mult)
                        v.reduce_sum(slotf[:, :], r8[:, :], AX.X)
                        v.tensor_tensor(slotf[:, :], slotf[:, :],
                                        p.RK32[:, tt:tt + 1], ALU.add)
                        v.tensor_scalar(slotf[:, :], slotf[:, :],
                                        float(NCH * CT - 1), 0.0, ALU.min, ALU.max)
                        v.tensor_copy(p.slots[:, tt:tt + 1], slotf[:, :],
                                      inc=(ssl2, tt + 1))

            @b.tensor
            def _(t):
                for tt in range(TT):
                    t.wait_ge(sln, tt + 1)
                    for dt in range(DT):
                        n = tt * DT + dt
                        if n >= 2:
                            t.wait_ge(scp8, n - 1)
                        t.transpose(trp[dt % 2][:, :],
                                    lnb[tt % 2][:, dt * P:(dt + 1) * P],
                                    p.ident_bf[:, :]).then_inc(smm8, 1)
                    if with_gate:
                        t.wait_ge(scp8, (tt + 1) * DT)
                        if tt >= 2:
                            t.wait_ge(sgv, tt - 1)
                        inst = None
                        for dt in range(DT):
                            inst = t.matmul(lgp[tt % 2][:, :],
                                            stag[tt % 2][:, dt * P:(dt + 1) * P],
                                            p.wg_sb[:, dt * E:(dt + 1) * E],
                                            start=(dt == 0), stop=(dt == DT - 1))
                        inst.then_inc(sgm, 1)
                        t.wait_ge(smk, tt + 1)
                        if tt >= 2:
                            t.wait_ge(ssl, tt - 1)
                        inst = None
                        for t2 in range(tt + 1):
                            inst = t.matmul(rkp[tt % 2][:, :],
                                            p.ones_bf[:, :] if t2 < tt else p.striu_bf[:, :],
                                            maskbf[:, t2 * E:(t2 + 1) * E],
                                            start=(t2 == 0), stop=(t2 == tt))
                        inst.then_inc(srk, 1)
                if with_gate:
                    t.wait_ge(sgv, TT)
                    t.wait_ge(smk, TT)
                    inst = None
                    for t2 in range(TT):
                        inst = t.matmul(lgp[0][:, :], p.ones_bf[:, :],
                                        maskbf[:, t2 * E:(t2 + 1) * E],
                                        start=(t2 == 0), stop=(t2 == TT - 1))
                    inst.then_inc(sld8, 1)
    p.phase_ctx.__exit__(None, None, None)


def emit_ffn(p, io, idx, acc, wait_lnT):
    nc = p.nc
    tag = f"ffn{idx}"
    p.phase_ctx2 = p.phase()
    p.phase_ctx2.__enter__()
    slc = [p.sem(f"{tag}_slc{i}") for i in range(2)]
    sw1 = p.gsem(f"{tag}_sw1")
    sw2 = p.gsem(f"{tag}_sw2")
    sb2 = p.gsem(f"{tag}_sb2")
    sb1 = p.sem(f"{tag}_sb1")
    stb1 = p.sem(f"{tag}_stb1")
    scb1 = p.sem(f"{tag}_scb1")
    sha = p.sem(f"{tag}_sha")
    shs = p.sem(f"{tag}_shs")
    sbm = p.sem(f"{tag}_sbm")
    sye = p.sem(f"{tag}_sye")
    sac = [p.gsem(f"{tag}_sac{i}") for i in range(2)]
    p.sem(f"{tag}_sch")
    p.sem(f"{tag}_vch")


    def amm(e, c, ht):
        return (e * CH + c) * HT + ht + 1

    def bmm(e, c, tt):
        return (e * CH + c) * CTT + tt + 1

    p.clear_block(tag)
    if True:
        w1 = p.sbuf(f"{tag}_w1", [P, DT * HID], BF16)
        w2 = p.sbuf(f"{tag}_w2", [P, HT * D], BF16)
        hT = p.sbuf(f"{tag}_hT", [P, HT * CT], BF16)
        lnc = [p.sbuf(f"{tag}_lnc{i}", [P, DT * CT], BF16) for i in range(2)]
        b2B = p.sbuf(f"{tag}_b2B", [P, D], F32)
        b1row = p.sbuf(f"{tag}_b1row", [HT, P], F32)
        b1col = p.sbuf(f"{tag}_b1col", [P, HT], F32)
        yst = [p.sbuf(f"{tag}_yst{i}", [P, D], F32) for i in range(2)]
        psa = [p.psum(f"{tag}_psa{i}", [P, CT], F32) for i in range(2)]
        psy = [p.psum(f"{tag}_psy{i}", [P, D], F32) for i in range(2)]
        psb1 = p.psum(f"{tag}_psb1", [P, HT], F32)

        with nc.Block(tag) as b:
            @b.gpsimd
            def _(g):
                def load_w1(e):
                    for q in range(4):
                        g.dma_start(
                            w1[:, :].rearrange("p (dt h) -> p dt h", dt=DT)
                            [:, :, q * 1024:(q + 1) * 1024],
                            io.w1[e].rearrange("(dt p) h -> p dt h", p=P)
                            [:, :, q * 1024:(q + 1) * 1024],
                        ).then_inc(sw1, 16)

                def load_w2(e):
                    for q in range(4):
                        g.dma_start(
                            w2[:, :].rearrange("p (ht d) -> p ht d", ht=HT)
                            [:, 8 * q:8 * (q + 1), :],
                            io.w2[e].rearrange("(ht p) d -> p ht d", p=P)
                            [:, 8 * q:8 * (q + 1), :],
                        ).then_inc(sw2, 16)

                load_w1(0)
                load_w2(0)
                for e in range(E):
                    if e > 0:
                        g.wait_ge(sye, bmm(e - 1, CH - 1, CTT - 1))
                        g.wait_ge(sac[0], 16 * (TT // 2) * e)
                        g.wait_ge(sac[1], 16 * (TT // 2) * e)
                    g.dma_start(b2B[:, :],
                                bass.AP(io.b2.tensor, e * D, [[0, P], [1, D]])
                                ).then_inc(sb2, 16)
                    for c in range(CH):
                        for tt in range(CTT):
                            gt = c * CTT + tt
                            g.wait_ge(sye, bmm(e, c, tt))
                            g.dma_start(acc[gt * P:(gt + 1) * P, :], yst[gt % 2][:, :],
                                        accum_op=ALU.add).then_inc(sac[gt % 2], 16)
                        if c == CH - 2 and e < E - 1:
                            g.wait_ge(sha, amm(e, CH - 1, HT - 1))
                            load_w1(e + 1)
                    if e < E - 1:
                        g.wait_ge(sbm, bmm(e, CH - 1, CTT - 1))
                        load_w2(e + 1)
                g.wait_ge(sac[0], 16 * (TT // 2) * E)
                g.wait_ge(sac[1], 16 * (TT // 2) * E)

            @b.sync
            def _(s):
                for e in range(E):
                    if e > 0:
                        s.wait_ge(stb1, e)
                    s.dma_start(b1row[:, :],
                                io.b1[e * HT:(e + 1) * HT, :]).then_inc(sb1, 16)
                    for c in range(CH):
                        k = e * CH + c
                        if k >= 2:
                            s.wait_ge(sha, amm((k - 2) // CH, (k - 2) % CH, HT - 1))
                        s.dma_start(
                            lnc[k % 2][:, :].rearrange("p (dt t) -> p dt t", dt=DT),
                            io.lnT.rearrange("(dt p) t -> p dt t", p=P)
                            [:, :, c * CT:(c + 1) * CT],
                        ).then_inc(slc[k % 2], 16)

            @b.tensor
            def _(t):
                for e in range(E):
                    t.wait_ge(sb1, 16 * (e + 1))
                    if e > 0:
                        t.wait_ge(scb1, e)
                    t.transpose(psb1[:, :], b1row[:, :],
                                p.ident_f32[0:HT, 0:HT]).then_inc(stb1, 1)
                    for c in range(CH):
                        k = e * CH + c
                        for ht in range(HT):
                            if ht == 0:
                                t.wait_ge(slc[k % 2], 16 * (k // 2 + 1))
                                t.wait_ge(sw1, 16 * 4 * (e + 1))
                            pr = k * HT + ht - 2
                            if pr >= 0:
                                t.wait_ge(shs, pr + 1)
                            last = None
                            for dt in range(DT):
                                last = t.matmul(
                                    psa[ht % 2][:, :],
                                    w1[:, dt * HID + ht * P: dt * HID + (ht + 1) * P],
                                    lnc[k % 2][:, dt * CT:(dt + 1) * CT],
                                    start=(dt == 0), stop=(dt == DT - 1))
                            last.then_inc(sha, 1)
                        for tt in range(CTT):
                            if tt == 0:
                                t.wait_ge(shs, (k + 1) * HT)
                                if c == 0:
                                    t.wait_ge(sw2, 16 * 4 * (e + 1))
                            bno = bmm(e, c, tt)
                            if bno >= 3:
                                t.wait_ge(sye, bno - 2)
                            gt = c * CTT + tt
                            last = None
                            for ht in range(HT):
                                for half in range(2):
                                    last = t.matmul(
                                        psy[gt % 2][:, half * 512:(half + 1) * 512],
                                        hT[:, ht * CT + tt * P: ht * CT + (tt + 1) * P],
                                        w2[:, ht * D + half * 512: ht * D + (half + 1) * 512],
                                        start=(ht == 0), stop=(ht == HT - 1))
                            last.then_inc(sbm, 1)

            @b.scalar
            def _(s0):
                s = p.chain(s0, f"{tag}_sch")
                for e in range(E):
                    s0.wait_ge(scb1, e + 1)
                    for c in range(CH):
                        k = e * CH + c
                        for ht in range(HT):
                            s0.wait_ge(sha, amm(e, c, ht))
                            if ht == 0 and k > 0:
                                s0.wait_ge(sbm, k * CTT)
                            s.activation(hT[:, ht * CT:(ht + 1) * CT], psa[ht % 2][:, :],
                                         AF.Silu, bias=b1col[:, ht:ht + 1],
                                         inc=(shs, amm(e, c, ht)))

            @b.vector
            def _(v0):
                v = p.chain(v0, f"{tag}_vch")
                for e in range(E):
                    v0.wait_ge(stb1, e + 1)
                    if e > 0:
                        v0.wait_ge(shs, e * CH * HT)
                    v.tensor_copy(b1col[:, :], psb1[:, :], inc=(scb1, e + 1))
                    v0.wait_ge(sb2, 16 * (e + 1))
                    for c in range(CH):
                        for tt in range(CTT):
                            bno = bmm(e, c, tt)
                            v0.wait_ge(sbm, bno)
                            gt = c * CTT + tt
                            if bno >= 3:
                                v0.wait_ge(sac[gt % 2], 16 * ((bno - 2 + 1) // 2))
                            v.tensor_tensor(yst[gt % 2][:, :], psy[gt % 2][:, :],
                                            b2B[:, :], ALU.add)
                            v.tensor_scalar(yst[gt % 2][:, :], yst[gt % 2][:, :],
                                            p.s8[:, gt * E + e: gt * E + e + 1], None,
                                            ALU.mult, inc=(sye, bno))
    p.phase_ctx2.__exit__(None, None, None)


def emit_ffn_sparse(p, io, idx, acc):
    nc = p.nc
    tag = f"ffn{idx}"
    p.phase_ctx2 = p.phase()
    p.phase_ctx2.__enter__()
    sw1 = p.gsem(f"{tag}_sw1")
    sw2 = p.gsem(f"{tag}_sw2")
    sb2 = p.gsem(f"{tag}_sb2")
    sbr = p.gsem(f"{tag}_sbr")
    slr = [p.gsem(f"{tag}_slr{i}") for i in range(CTT)]
    sac = [p.gsem(f"{tag}_sac{i}") for i in range(2)]
    sga = [p.gsem(f"{tag}_sga{i}") for i in range(2)]
    stf = [p.sem(f"{tag}_stf{i}") for i in range(CTT)]
    sod = [p.sem(f"{tag}_sod{i}") for i in range(2)]
    stb1 = p.sem(f"{tag}_stb1")
    scb1 = p.sem(f"{tag}_scb1")
    sstr = p.sem(f"{tag}_sstr")
    scg = p.sem(f"{tag}_scg")
    sha = p.sem(f"{tag}_sha")
    shs = p.sem(f"{tag}_shs")
    sbm = p.sem(f"{tag}_sbm")
    soe = p.sem(f"{tag}_soe")
    syw = p.sem(f"{tag}_syw")
    p.sem(f"{tag}_sch")
    p.sem(f"{tag}_vch")

    def amm(k, ht):
        return k * HT + ht + 1

    def bmm(k, tl):
        return k * CTT + tl + 1

    p.clear_block(tag)
    if True:
        w1 = p.sbuf(f"{tag}_w1", [P, DT * HID], BF16)
        w2 = p.sbuf(f"{tag}_w2", [P, HT * D], BF16)
        hT = p.sbuf(f"{tag}_hT", [P, HT * CT], BF16)
        lnGst = p.sbuf(f"{tag}_lnGst", [P, CTT * D], BF16)
        lnGT = p.sbuf(f"{tag}_lnGT", [P, DT * CT], BF16)
        b2B = p.sbuf(f"{tag}_b2B", [P, D], F32)
        b1row = p.sbuf(f"{tag}_b1row", [HT, P], F32)
        b1col = p.sbuf(f"{tag}_b1col", [P, HT], F32)
        tkf = [p.sbuf(f"{tag}_tkf{i}", [P, CTT], I32) for i in range(2)]
        ost = [p.sbuf(f"{tag}_ost{i}", [P, D], F32) for i in range(2)]
        gst = [p.sbuf(f"{tag}_gst{i}", [P, D], F32) for i in range(2)]
        yst = ost
        psa = [p.psum(f"{tag}_psa{i}", [P, CT], F32) for i in range(2)]
        psy = [p.psum(f"{tag}_psy{i}", [P, D], F32) for i in range(2)]
        trp2 = [p.psum(f"{tag}_trp2{i}", [P, P], BF16) for i in range(2)]

        with nc.Block(tag) as b:
            @b.gpsimd
            def _(g):
                for k in range(NCH):
                    if k >= 1:
                        g.wait_ge(sstr, 32 * k)
                    for tl in range(CTT):
                        g.wait_ge(stf[tl], 16 * (k + 1))
                        g.indirect_dma_start(
                            lnGst[:, tl * D:(tl + 1) * D], None, io.lnR,
                            bass.IndirectOffsetOnAxis(tkf[k % 2][:, tl:tl + 1], 0)
                        ).then_inc(slr[tl], 16)
                    if k > 0:
                        g.wait_ge(stb1, k)
                    g.indirect_dma_start(
                        b1row[:, :], None, io.b1,
                        bass.IndirectOffsetOnAxis(p.IDXB1[0:HT, k:k + 1], 0)
                    ).then_inc(sbr, 16)
                    if k > 0:
                        g.wait_ge(sha, amm(k - 1, HT - 1))
                    g.indirect_dma_start(
                        w1[:, :], None, io.W1bf,
                        bass.IndirectOffsetOnAxis(p.IDXW[:, k:k + 1], 0)
                    ).then_inc(sw1, 16)
                    if k > 0:
                        g.wait_ge(soe, bmm(k - 1, CTT - 1))
                    g.indirect_dma_start(
                        b2B[:, :], None, io.b2,
                        bass.IndirectOffsetOnAxis(p.IDXE[:, k:k + 1], 0)
                    ).then_inc(sb2, 16)
                    if k > 0:
                        g.wait_ge(sbm, bmm(k - 1, CTT - 1))
                    g.indirect_dma_start(
                        w2[:, :], None, io.W2bf,
                        bass.IndirectOffsetOnAxis(p.IDXW[:, k:k + 1], 0)
                    ).then_inc(sw2, 16)
                g.wait_ge(sod[0], 16 * (NCH * CTT // 2))
                g.wait_ge(sod[1], 16 * (NCH * CTT // 2))
                for tt in range(TT):
                    if tt >= 2:
                        g.wait_ge(syw, tt - 1)
                    g.indirect_dma_start(
                        gst[tt % 2][:, :], None,
                        io.o_sp,
                        bass.IndirectOffsetOnAxis(p.slots[:, tt:tt + 1], 0),
                    ).then_inc(sga[tt % 2], 16)
                    if tt >= 1:
                        g.wait_ge(syw, tt)
                        g.dma_start(acc[(tt - 1) * P:tt * P, :], yst[(tt - 1) % 2][:, :],
                                    accum_op=ALU.add).then_inc(sac[(tt - 1) % 2], 16)
                g.wait_ge(syw, TT)
                g.dma_start(acc[(TT - 1) * P:TT * P, :], yst[(TT - 1) % 2][:, :],
                            accum_op=ALU.add).then_inc(sac[(TT - 1) % 2], 16)
                g.wait_ge(sac[0], 16 * (TT // 2))
                g.wait_ge(sac[1], 16 * (TT // 2))

            @b.sync
            def _(s):
                for k in range(NCH):
                    for tl in range(CTT):
                        if k >= 2:
                            s.wait_ge(slr[tl], 16 * (k - 1))
                        s.dma_start(tkf[k % 2][:, tl:tl + 1],
                                    bass.AP(io.tokof.tensor, k * CT + tl * P,
                                            [[1, P], [1, 1]])
                                    ).then_inc(stf[tl], 16)
                    for tl in range(CTT):
                        s.wait_ge(soe, bmm(k, tl))
                        s.dma_start(io.o_sp[(k * CTT + tl) * P:(k * CTT + tl + 1) * P, :],
                                    ost[tl % 2][:, :]).then_inc(sod[tl % 2], 16)

            @b.tensor
            def _(t):
                for k in range(NCH):
                    t.wait_ge(sbr, 16 * (k + 1))
                    if k > 0:
                        t.wait_ge(scb1, k)
                        t.wait_ge(shs, amm(k - 1, HT - 1))
                    t.transpose(psa[0][:, 0:HT], b1row[:, :],
                                p.ident_f32[0:HT, 0:HT]).then_inc(stb1, 1)
                    for tl in range(CTT):
                        t.wait_ge(slr[tl], 16 * (k + 1))
                    if k >= 1:
                        t.wait_ge(scg, 32 * k)
                    for n in range(HT):
                        tl, dt = divmod(n, DT)
                        if 32 * k + n >= 2:
                            t.wait_ge(scg, 32 * k + n - 1)
                        t.transpose(trp2[n % 2][:, :],
                                    lnGst[:, tl * D + dt * P: tl * D + (dt + 1) * P],
                                    p.ident_bf[:, :]).then_inc(sstr, 1)
                    for ht in range(HT):
                        if ht == 0:
                            t.wait_ge(scg, 32 * (k + 1))
                            t.wait_ge(sw1, 16 * (k + 1))
                        pr = k * HT + ht - 2
                        if pr >= 0:
                            t.wait_ge(shs, pr + 1)
                        last = None
                        for dt in range(DT):
                            last = t.matmul(
                                psa[ht % 2][:, :],
                                w1[:, dt * HID + ht * P: dt * HID + (ht + 1) * P],
                                lnGT[:, dt * CT:(dt + 1) * CT],
                                start=(dt == 0), stop=(dt == DT - 1))
                        last.then_inc(sha, 1)
                    for tl in range(CTT):
                        if tl == 0:
                            t.wait_ge(shs, (k + 1) * HT)
                            t.wait_ge(sw2, 16 * (k + 1))
                        bno = bmm(k, tl)
                        if bno >= 3:
                            t.wait_ge(soe, bno - 2)
                        last = None
                        for ht in range(HT):
                            for half in range(2):
                                last = t.matmul(
                                    psy[tl % 2][:, half * 512:(half + 1) * 512],
                                    hT[:, ht * CT + tl * P: ht * CT + (tl + 1) * P],
                                    w2[:, ht * D + half * 512: ht * D + (half + 1) * 512],
                                    start=(ht == 0), stop=(ht == HT - 1))
                        last.then_inc(sbm, 1)

            @b.scalar
            def _(s0):
                s = p.chain(s0, f"{tag}_sch")
                for k in range(NCH):
                    s0.wait_ge(scb1, k + 1)
                    for ht in range(HT):
                        s0.wait_ge(sha, amm(k, ht))
                        if ht == 0 and k > 0:
                            s0.wait_ge(sbm, k * CTT)
                        s.activation(hT[:, ht * CT:(ht + 1) * CT], psa[ht % 2][:, :],
                                     AF.Silu, bias=b1col[:, ht:ht + 1],
                                     inc=(shs, amm(k, ht)))

            @b.vector
            def _(v0):
                v = p.chain(v0, f"{tag}_vch")
                for k in range(NCH):
                    v0.wait_ge(stb1, k + 1)
                    if k > 0:
                        v0.wait_ge(shs, amm(k - 1, HT - 1))
                    v.tensor_copy(b1col[:, :], psa[0][:, 0:HT], inc=(scb1, k + 1))
                    v0.wait_ge(sb2, 16 * (k + 1))
                    if k >= 1:
                        v0.wait_ge(sha, 32 * k)
                    for n in range(HT):
                        tl, dt = divmod(n, DT)
                        v0.wait_ge(sstr, 32 * k + n + 1)
                        v.tensor_copy(lnGT[:, dt * CT + tl * P: dt * CT + (tl + 1) * P],
                                      trp2[n % 2][:, :], inc=(scg, 32 * k + n + 1))
                    for tl in range(CTT):
                        bno = bmm(k, tl)
                        v0.wait_ge(sbm, bno)
                        if bno >= 3:
                            v0.wait_ge(sod[tl % 2], 16 * ((bno - 2 + 1) // 2))
                        v.tensor_tensor(ost[tl % 2][:, :], psy[tl % 2][:, :],
                                        b2B[:, :], ALU.add, inc=(soe, bno))
                for tt in range(TT):
                    v0.wait_ge(sga[tt % 2], 16 * (tt // 2 + 1))
                    if tt >= 2:
                        v0.wait_ge(sac[tt % 2], 16 * (tt // 2))
                    v.tensor_scalar(yst[tt % 2][:, :], gst[tt % 2][:, :],
                                    p.halfw[:, tt:tt + 1], None, ALU.mult,
                                    inc=(syw, tt + 1))
    p.phase_ctx2.__exit__(None, None, None)


def emit_attention(p, io, wait_lnT):
    nc = p.nc
    p.phase_ctx3 = p.phase()
    p.phase_ctx3.__enter__()
    sqlc = [p.sem(f"at_sqlc{i}") for i in range(2)]
    swv = p.gsem("at_swv")
    sbv = p.sem("at_sbv")
    sbt = p.sem("at_sbt")
    sbc = p.sem("at_sbc")
    sall = p.sem("at_sall")
    sevk = p.sem("at_sevk")
    sevv = p.sem("at_sevv")
    sqs = p.sem("at_sqs")
    sqh = [p.sem(f"at_sqh{i}") for i in range(2)]
    sp1m = p.sem("at_sp1m")
    sp1v = p.sem("at_sp1v")
    sp1e = p.sem("at_sp1e")
    sfd = p.sem("at_sfd")
    sp1l = p.sem("at_sp1l")
    saugt = p.sem("at_saugt")
    saugc = p.sem("at_saugc")
    sbb = [p.sem(f"at_sbb{i}") for i in range(4)]
    slv = [p.sem(f"at_slv{i}") for i in range(2)]
    sp2m = p.sem("at_sp2m")
    sp2v = p.sem("at_sp2v")
    sp2e = p.sem("at_sp2e")
    sp2c = p.sem("at_sp2c")
    sctxv = p.sem("at_sctxv")
    scst = [p.sem(f"at_scst{i}") for i in range(2)]
    swo = p.gsem("at_swo")
    sbo = p.sem("at_sbo")
    scl = [p.sem(f"at_scl{i}") for i in range(2)]
    som = p.sem("at_som")
    soe = p.sem("at_soe")
    soa = [p.gsem(f"at_soa{i}") for i in range(2)]
    for _cn in ("at_qkv_sch", "at_qkv_vch", "at_core_vch", "at_core_sch", "at_op_vch"):
        p.sem(_cn)

    KTB = []
    run = 0
    for h in range(NH):
        row = []
        for qb in range(CH):
            row.append(run)
            run += 4 * qb + 4
        KTB.append(row)
    KT_TOTAL = run
    DIAG = {}
    _d = 0
    for _h in range(NH):
        for _qb in range(CH):
            for _kt in range(4 * _qb, 4 * _qb + 4):
                _d += 1
                DIAG[(_h, _qb, _kt)] = _d

    p.clear_block("at")
    if True:
        kT = p.sbuf("at_kT", [P, DT * T], BF16)
        v_sb = p.sbuf("at_v", [P, TT * D], BF16)

        sub = contextlib.ExitStack()
        old = p.pstack
        p.pstack = sub
        wq = p.sbuf("at_wq", [P, DT * D], BF16)
        wk = p.sbuf("at_wk", [P, DT * D], BF16)
        wv = p.sbuf("at_wv", [P, DT * D], BF16)
        lnc = [p.sbuf(f"at_lnc{i}", [P, DT * CT], BF16) for i in range(2)]
        qstage = p.sbuf("at_qst", [P, DT * CT], BF16)
        b8r = p.sbuf("at_b8r", [DT, 2 * P], F32)
        bqcol = p.sbuf("at_bqcol", [P, DT], F32)
        bkcol = p.sbuf("at_bkcol", [P, DT], F32)
        bvB = p.sbuf("at_bvB", [P, D], BF16)
        qsc = p.sbuf("at_qsc", [P, 1], F32)
        pk = [p.psum(f"at_pk{i}", [P, 512], F32) for i in range(2)]
        pv = [p.psum(f"at_pv{i}", [P, 512], F32) for i in range(2)]
        pbq = p.psum("at_pbq", [P, DT], F32)
        pbk = p.psum("at_pbk", [P, DT], F32)
        p.pstack = old

        with nc.Block("at_qkv") as b:
            @b.gpsimd
            def _(g):
                for w_sb, w_hbm in ((wq, io.wq), (wk, io.wk), (wv, io.wv)):
                    g.dma_start(
                        w_sb[:, :].rearrange("p (dt d) -> p dt d", dt=DT),
                        w_hbm.rearrange("(dt p) d -> p dt d", p=P)).then_inc(swv, 16)
                g.dma_start(bvB[:, :],
                            bass.AP(io.bv.tensor, 0, [[0, P], [1, D]])).then_inc(swv, 16)

            @b.sync
            def _(s):
                s.dma_start(b8r[:, 0:P],
                            io.bq.rearrange("(dt p) -> dt p", p=P)).then_inc(sbv, 16)
                s.dma_start(b8r[:, P:2 * P],
                            io.bk.rearrange("(dt p) -> dt p", p=P)).then_inc(sbv, 16)

                for c in range(CH):
                    if c >= 2:
                        s.wait_ge(sall, (c - 1) * 24)
                    s.dma_start(
                        lnc[c % 2][:, :].rearrange("p (dt t) -> p dt t", dt=DT),
                        io.lnT.rearrange("(dt p) t -> p dt t", p=P)
                        [:, :, c * CT:(c + 1) * CT],
                    ).then_inc(sqlc[c % 2], 16)
                    s.wait_ge(sevk, c * 16 + 16)
                    s.dma_start(
                        io.qT.rearrange("(dt p) t -> p dt t", p=P)
                        [:, :, c * CT:(c + 1) * CT],
                        qstage[:, :].rearrange("p (dt t) -> p dt t", dt=DT),
                    ).then_inc(sqs, 16)
                s.wait_ge(sqs, 16 * CH)

            @b.tensor
            def _(t):
                t.wait_ge(sbv, 32)
                t.transpose(pbq[:, :], b8r[:, 0:P], p.ident_f32[0:DT, 0:DT])
                t.transpose(pbk[:, :], b8r[:, P:2 * P],
                            p.ident_f32[0:DT, 0:DT]).then_inc(sbt, 1)
                t.wait_ge(swv, 64)
                for c in range(CH):
                    t.wait_ge(sqlc[c % 2], 16 * (c // 2 + 1))
                    for grp in range(16):
                        w_sb = wk if grp < 8 else wq
                        dout = grp % 8
                        gidx = c * 16 + grp
                        if gidx >= 2:
                            t.wait_ge(sevk, gidx - 1)
                        last = None
                        for din in range(DT):
                            last = t.matmul(
                                pk[grp % 2][:, :],
                                w_sb[:, din * D + dout * P: din * D + (dout + 1) * P],
                                lnc[c % 2][:, din * CT:(din + 1) * CT],
                                start=(din == 0), stop=(din == DT - 1))
                        last.then_inc(sall, 1)
                    for grp in range(8):
                        tt, half = divmod(grp, 2)
                        gidx = c * 8 + grp
                        if gidx >= 2:
                            t.wait_ge(sevv, gidx - 1)
                        last = None
                        for din in range(DT):
                            last = t.matmul(
                                pv[grp % 2][:, :],
                                lnc[c % 2][:, din * CT + tt * P: din * CT + (tt + 1) * P],
                                wv[:, din * D + half * 512: din * D + (half + 1) * 512],
                                start=(din == 0), stop=(din == DT - 1))
                        last.then_inc(sall, 1)

            @b.scalar
            def _(s0):
                s = p.chain(s0, "at_qkv_sch")
                s0.wait_ge(sbc, 1)
                for c in range(CH):
                    for grp in range(16):
                        dout = grp % 8
                        s0.wait_ge(sall, c * 24 + grp + 1)
                        if grp < 8:
                            s.activation(kT[:, dout * T + c * CT: dout * T + (c + 1) * CT],
                                         pk[grp % 2][:, :], AF.Identity,
                                         bias=bkcol[:, dout:dout + 1],
                                         inc=(sevk, c * 16 + grp + 1))
                        else:
                            if c >= 1 and grp == 8:
                                s0.wait_ge(sqs, 16 * c)
                            s.activation(qstage[:, dout * CT:(dout + 1) * CT],
                                         pk[grp % 2][:, :], AF.Identity,
                                         bias=bqcol[:, dout:dout + 1],
                                         scale=qsc[:, :],
                                         inc=(sevk, c * 16 + grp + 1))

            @b.vector
            def _(v0):
                v = p.chain(v0, "at_qkv_vch")
                v0.wait_ge(sbt, 1)
                v.tensor_copy(bqcol[:, :], pbq[:, :])
                v.tensor_scalar(bqcol[:, :], bqcol[:, :], 0.125, None, ALU.mult)
                v.tensor_copy(bkcol[:, :], pbk[:, :])
                v.memset(qsc[:, :], 0.125, inc=(sbc, 1))
                v0.wait_ge(swv, 64)
                for c in range(CH):
                    for grp in range(8):
                        tt, half = divmod(grp, 2)
                        v0.wait_ge(sall, c * 24 + 16 + grp + 1)
                        v.tensor_tensor(
                            v_sb[:, (c * CTT + tt) * D + half * 512:
                                 (c * CTT + tt) * D + (half + 1) * 512],
                            pv[grp % 2][:, :], bvB[:, half * 512:(half + 1) * 512],
                            ALU.add, inc=(sevv, c * 8 + grp + 1))
        sub.close()

        sub = contextlib.ExitStack()
        old = p.pstack
        p.pstack = sub
        qh = p.sbuf("at_qh", [P, T], BF16)
        p1scr = p.sbuf("at_p1scr", [P, 1024], BF16)
        pT = [p.sbuf(f"at_pT{i}", [P, 512], BF16) for i in range(2)]
        ctxp = [p.sbuf(f"at_ctxp{i}", [P, T], BF16) for i in range(2)]
        LINV32 = p.sbuf("at_LINV32", [P, TT], F32)
        LRUN32 = p.sbuf("at_LRUN32", [P, TT], F32)
        nc20 = p.nc20
        linvrow = p.sbuf("at_linvrow", [4, P], F32)
        linvB = [p.sbuf(f"at_linvB{i}", [HD, CT], F32) for i in range(2)]
        cs32 = p.sbuf("at_cs32", [P, 32], F32)
        psA = p.psum("at_psA", [P, T], F32)
        p.pstack = old

        p1s = [psA[:, 0:1024], psA[:, 1024:2048]]
        psT = [psA[:, 2048:2560], psA[:, 2560:3072]]
        psctx = [psA[0:HD, 3072:3584], psA[0:HD, 3584:4096]]
        pli = psA[0:4, 2048:2048 + P]

        def p1_chunks(qt):
            km = (qt + 1) * P
            return [(j, j * 1024, min(1024, km - j * 1024))
                    for j in range((km + 1023) // 1024)]

        CG = {}
        _c = 0
        for _h in range(NH):
            for _qt in range(TT):
                for (_j, _o, _w) in p1_chunks(_qt):
                    _c += 1
                    CG[(_h, _qt, _j)] = _c
        KTC = {}
        _c = 0
        for _h in range(NH):
            for _qb in range(CH):
                for _kt in range(4 * _qb + 4):
                    _c += 1
                    KTC[(_h, _qb, _kt)] = _c

        def kt_width(qb, kt):
            s = kt - 4 * qb
            return 512 - s * P if s >= 0 else 512

        with nc.Block("at_core") as b:
            @b.sync
            def _(s):
                def store_pair(pr):
                    s.wait_ge(sctxv, (2 * pr + 2) * 8)
                    s.dma_start(io.ctxT[pr * P:(pr + 1) * P, :],
                                ctxp[pr % 2][:, :]).then_inc(scst[pr % 2], 16)

                def bcast_slot(gqb):
                    s.wait_ge(saugc, gqb + 1)
                    if gqb >= 2:
                        for j in range(4):
                            s.wait_ge(sbb[j], 16 * (gqb - 1))
                    s.dma_start(io.linv_dr[(gqb % 2) * 4:(gqb % 2) * 4 + 4, :],
                                linvrow[:, :]).then_inc(slv[gqb % 2], 16)
                    s.wait_ge(slv[gqb % 2], 16 * (gqb // 2 + 1))
                    if gqb >= 2:
                        s.wait_ge(sctxv, gqb - 1)
                    for j in range(4):
                        s.dma_start(
                            linvB[gqb % 2][:, j * P:(j + 1) * P],
                            bass.AP(io.linv_dr.tensor,
                                    ((gqb % 2) * 4 + j) * P,
                                    [[0, HD], [1, P]]),
                        ).then_inc(sbb[j], 16)

                for h in range(NH):
                    if h >= 2:
                        s.wait_ge(sctxv, (h - 1) * 8)
                    s.dma_start(qh[(h % 2) * HD:(h % 2 + 1) * HD, :],
                                io.qT[h * HD:(h + 1) * HD, :]).then_inc(sqh[h % 2], 16)
                    if h >= 1:
                        for qb in range(CH):
                            bcast_slot((h - 1) * CH + qb)
                    if h >= 3 and h % 2 == 1:
                        store_pair((h - 3) // 2)
                for qb in range(CH):
                    bcast_slot((NH - 1) * CH + qb)
                store_pair(NH // 2 - 1)
                s.wait_ge(scst[0], 16 * (NH // 4))
                s.wait_ge(scst[1], 16 * (NH // 4))

            @b.tensor
            def _(t):
                def p1_mms_qt(h, qt):
                    kbase = (h % 2) * HD
                    kcol = (h // 2) * T
                    if qt == 0:
                        t.wait_ge(sqh[h % 2], 16 * (h // 2 + 1))
                    chunks = p1_chunks(qt)
                    for (j, off, w) in chunks:
                        cg = CG[(h, qt, j)]
                        dl = qt * P - off if j == len(chunks) - 1 else -1
                        if cg >= 3:
                            t.wait_ge(sp1e, cg - 2)
                        last = None
                        for j2 in range(0, w, 512):
                            w2 = min(512, w - j2)
                            has_diag = j2 <= dl < j2 + w2
                            last = t.matmul(
                                p1s[cg % 2][:, j2:j2 + w2],
                                qh[kbase:kbase + HD, qt * P:(qt + 1) * P],
                                kT[kbase:kbase + HD,
                                   kcol + off + j2: kcol + off + j2 + w2],
                                start=True, stop=not has_diag)
                            if has_diag:
                                last = t.matmul(p1s[cg % 2][:, dl:dl + P],
                                                p.ident_bf[:, :], p.mneg1[:, :],
                                                start=False, stop=True)
                        last.then_inc(sp1m, 1)

                def p2_mms_qb(h, qb):
                    kbase = (h % 2) * HD
                    kcol = (h // 2) * T
                    gqb = h * CH + qb
                    t.wait_ge(sp1l, h * TT + 4 * qb + 4)
                    t.wait_ge(sctxv, gqb)
                    if gqb >= 1:
                        t.wait_ge(saugc, gqb)
                        t.wait_ge(sp2e, KTB[h][qb])
                    t.transpose(pli, LINV32[:, 4 * qb:4 * qb + 4],
                                p.ident_f32[:, :]).then_inc(saugt, 1)
                    nkt = 4 * qb + 4

                    def ctx_mm(kt):
                        cnt = KTC[(h, qb, kt)]
                        w = kt_width(qb, kt)
                        sb_ = kt - 4 * qb
                        qoff = sb_ * P if sb_ >= 0 else 0
                        t.wait_ge(sp2e, cnt)
                        t.matmul(psctx[qb % 2][:, qoff:512],
                                 v_sb[:, kt * D + h * HD: kt * D + (h + 1) * HD],
                                 pT[kt % 2][:, 0:w],
                                 start=(kt == 0), stop=(kt == nkt - 1)
                                 ).then_inc(sp2c, 1)

                    for kt in range(nkt):
                        cnt = KTC[(h, qb, kt)]
                        w = kt_width(qb, kt)
                        sb_ = kt - 4 * qb
                        qoff = sb_ * P if sb_ >= 0 else 0
                        if kt == 0:
                            t.wait_ge(saugc, gqb + 1)
                        if cnt >= 3:
                            t.wait_ge(sp2e, cnt - 2)
                        inst = t.matmul(psT[kt % 2][:, 0:w],
                                        kT[kbase:kbase + HD,
                                           kcol + kt * P: kcol + (kt + 1) * P],
                                        qh[kbase:kbase + HD,
                                           qb * CT + qoff:(qb + 1) * CT],
                                        start=True, stop=(sb_ < 0))
                        if sb_ >= 0:
                            inst = t.matmul(psT[kt % 2][:, 0:P],
                                            p.ident_bf[:, :], p.mnegT[:, :],
                                            start=False, stop=True)
                        inst.then_inc(sp2m, 1)
                        if kt > 0:
                            ctx_mm(kt - 1)
                    ctx_mm(nkt - 1)

                for qt in range(TT):
                    p1_mms_qt(0, qt)
                for h in range(NH):
                    for qb in range(CH):
                        if h + 1 < NH:
                            for qt in range(4 * qb, 4 * qb + 4):
                                p1_mms_qt(h + 1, qt)
                        p2_mms_qb(h, qb)

            @b.vector
            def _(v0):
                v = p.chain(v0, "at_core_vch")

                def p1_v_qt(h, qt):
                    gqt = h * TT + qt
                    chunks = p1_chunks(qt)
                    v0.wait_ge(sp1l, max(gqt + 1 - TT, 0))
                    if h >= 1:
                        v0.wait_ge(saugt, (h - 1) * CH + qt // 4 + 1)
                    for (j, off, w) in chunks:
                        cg = CG[(h, qt, j)]
                        v0.wait_ge(sp1e, cg)
                        lr = LRUN32[:, qt:qt + 1]
                        if j == 0:
                            v.tensor_copy(lr, cs32[:, cg % 32:cg % 32 + 1],
                                          inc=(sfd, cg))
                        else:
                            v.tensor_tensor(lr, lr,
                                            cs32[:, cg % 32:cg % 32 + 1], ALU.add,
                                            inc=(sfd, cg))
                    v.reciprocal(LINV32[:, qt:qt + 1], LRUN32[:, qt:qt + 1],
                                 inc=(sp1l, gqt + 1))

                def p2_v_pre(h, qb):
                    gqb = h * CH + qb
                    v0.wait_ge(saugt, gqb + 1)
                    if gqb >= 1:
                        v0.wait_ge(slv[(gqb - 1) % 2], 16 * ((gqb - 1) // 2 + 1))
                    v.tensor_copy(linvrow[:, :], pli, inc=(saugc, gqb + 1))

                def p2_v_post(h, qb):
                    gqb = h * CH + qb
                    nkt = 4 * qb + 4
                    v0.wait_ge(sp2c, KTC[(h, qb, nkt - 1)])
                    for j in range(4):
                        v0.wait_ge(sbb[j], 16 * (gqb + 1))
                    if qb == 0 and h % 2 == 0 and h >= 4:
                        v0.wait_ge(scst[(h // 2) % 2], 16 * (h // 4))
                    v.tensor_tensor(ctxp[(h // 2) % 2][(h % 2) * HD:(h % 2 + 1) * HD,
                                                       qb * CT:(qb + 1) * CT],
                                    psctx[qb % 2], linvB[gqb % 2][:, :], ALU.mult,
                                    inc=(sctxv, gqb + 1))

                for qt in range(TT):
                    p1_v_qt(0, qt)
                for h in range(NH):
                    for qb in range(CH):
                        p2_v_pre(h, qb)
                        if h + 1 < NH:
                            for qt in range(4 * qb, 4 * qb + 4):
                                p1_v_qt(h + 1, qt)
                        p2_v_post(h, qb)

            @b.scalar
            def _(s0):
                s = p.chain(s0, "at_core_sch")

                def p1_s_exps(h, qt):
                    chunks = p1_chunks(qt)
                    for (j, off, w) in chunks:
                        cg = CG[(h, qt, j)]
                        s0.wait_ge(sp1m, cg)
                        if cg >= 33:
                            s0.wait_ge(sfd, cg - 32)
                        s.activation(p1scr[:, 0:w], p1s[cg % 2][:, 0:w], AF.Exp,
                                     bias=nc20[:, :],
                                     accum_out=cs32[:, cg % 32:cg % 32 + 1],
                                     inc=(sp1e, cg))

                def p1_s_qt(h, qt):
                    p1_s_exps(h, qt)

                def p2_s_qb(h, qb):
                    nkt = 4 * qb + 4
                    for kt in range(nkt):
                        cnt = KTC[(h, qb, kt)]
                        w = kt_width(qb, kt)
                        s0.wait_ge(sp2m, cnt)
                        if cnt >= 3:
                            s0.wait_ge(sp2c, cnt - 2)
                        s.activation(pT[kt % 2][:, 0:w], psT[kt % 2][:, 0:w],
                                     AF.Exp, bias=nc20[:, :], inc=(sp2e, cnt))

                for qt in range(TT):
                    p1_s_qt(0, qt)
                for h in range(NH):
                    for qb in range(CH):
                        if h + 1 < NH:
                            for qt in range(4 * qb, 4 * qb + 4):
                                p1_s_exps(h + 1, qt)
                        p2_s_qb(h, qb)

        sub.close()
        sub = contextlib.ExitStack()
        old = p.pstack
        p.pstack = sub
        wo = p.sbuf("at_wo", [P, DT * D], BF16)
        boB = p.sbuf("at_boB", [P, D], F32)
        cstage = [p.sbuf(f"at_cst{i}", [P, DT * P], BF16) for i in range(2)]
        yo = [p.sbuf(f"at_yo{i}", [P, D], F32) for i in range(2)]
        po = [p.psum(f"at_po{i}", [P, D], F32) for i in range(2)]
        p.pstack = old

        with nc.Block("at_outproj") as b:
            @b.gpsimd
            def _(g):
                g.dma_start(wo[:, :].rearrange("p (dt d) -> p dt d", dt=DT),
                            io.wo.rearrange("(dt p) d -> p dt d", p=P)).then_inc(swo, 16)
                for tt in range(TT):
                    g.wait_ge(soe, tt + 1)
                    g.dma_start(io.xwork[tt * P:(tt + 1) * P, :], yo[tt % 2][:, :],
                                accum_op=ALU.add).then_inc(soa[tt % 2], 16)
                g.wait_ge(soa[0], 16 * (TT // 2))
                g.wait_ge(soa[1], 16 * (TT // 2))

            @b.sync
            def _(s):
                s.dma_start(boB[:, :],
                            bass.AP(io.bo.tensor, 0, [[0, P], [1, D]])).then_inc(sbo, 16)
                for tt in range(TT):
                    if tt >= 2:
                        s.wait_ge(som, tt - 1)
                    s.dma_start(
                        cstage[tt % 2][:, :].rearrange("p (pr t) -> p pr t", pr=DT),
                        io.ctxT.rearrange("(pr p) t -> p pr t", p=P)
                        [:, :, tt * P:(tt + 1) * P],
                    ).then_inc(scl[tt % 2], 16)

            @b.tensor
            def _(t):
                t.wait_ge(swo, 16)
                for tt in range(TT):
                    t.wait_ge(scl[tt % 2], 16 * (tt // 2 + 1))
                    if tt >= 2:
                        t.wait_ge(soe, tt - 1)
                    last = None
                    for dm in range(DT):
                        for half in range(2):
                            last = t.matmul(
                                po[tt % 2][:, half * 512:(half + 1) * 512],
                                cstage[tt % 2][:, dm * P:(dm + 1) * P],
                                wo[:, dm * D + half * 512: dm * D + (half + 1) * 512],
                                start=(dm == 0), stop=(dm == DT - 1))
                    last.then_inc(som, tt + 1) if False else last.then_inc(som, 1)

            @b.vector
            def _(v0):
                v = p.chain(v0, "at_op_vch")
                v0.wait_ge(sbo, 16)
                for tt in range(TT):
                    v0.wait_ge(som, tt + 1)
                    if tt >= 2:
                        v0.wait_ge(soa[tt % 2], 16 * (tt // 2))
                    v.tensor_tensor(yo[tt % 2][:, :], po[tt % 2][:, :], boB[:, :],
                                    ALU.add, inc=(soe, tt + 1))
        sub.close()
    p.phase_ctx3.__exit__(None, None, None)


class IO:
    pass


def build_io(nc):
    io = IO()
    io.x_in = nc.dram_tensor("x", [T, D], F32, kind="ExternalInput").ap()
    io.wg = nc.dram_tensor("Wg", [D, E], F32, kind="ExternalInput").ap()
    io.bg = nc.dram_tensor("bg", [E], F32, kind="ExternalInput").ap()
    io.w1 = nc.dram_tensor("W1", [E, D, HID], F32, kind="ExternalInput").ap()
    io.b1 = nc.dram_tensor("b1", [E * HT, P], F32, kind="ExternalInput").ap()
    io.w2 = nc.dram_tensor("W2", [E, HID, D], F32, kind="ExternalInput").ap()
    io.b2 = nc.dram_tensor("b2", [E, D], F32, kind="ExternalInput").ap()
    io.wq = nc.dram_tensor("Wq", [D, D], F32, kind="ExternalInput").ap()
    io.bq = nc.dram_tensor("bq", [D], F32, kind="ExternalInput").ap()
    io.wk = nc.dram_tensor("Wk", [D, D], F32, kind="ExternalInput").ap()
    io.bk = nc.dram_tensor("bk", [D], F32, kind="ExternalInput").ap()
    io.wv = nc.dram_tensor("Wv", [D, D], F32, kind="ExternalInput").ap()
    io.bv = nc.dram_tensor("bv", [D], F32, kind="ExternalInput").ap()
    io.wo = nc.dram_tensor("Wo", [D, D], F32, kind="ExternalInput").ap()
    io.bo = nc.dram_tensor("bo", [D], F32, kind="ExternalInput").ap()
    io.y = nc.dram_tensor("y", [T, D], F32, kind="ExternalOutput").ap()
    io.xwork = nc.dram_tensor("xwork", [T, D], F32, kind="Internal").ap()
    io.o_sp = nc.dram_tensor("o_sp", [NCH * CT, D], F32, kind="Internal").ap()
    io.lnR = nc.dram_tensor("lnR", [T, D], BF16, kind="Internal").ap()
    io.tokof = nc.dram_tensor("tokof", [NCH * CT, 1], I32, kind="Internal").ap()
    io.W1bf = nc.dram_tensor("W1bf", [E * P, DT * HID], BF16, kind="Internal").ap()
    io.W2bf = nc.dram_tensor("W2bf", [E * P, HT * D], BF16, kind="Internal").ap()
    io.lnT = nc.dram_tensor("lnT", [D, T], BF16, kind="Internal").ap()
    io.qT = nc.dram_tensor("qTs", [D, T], BF16, kind="Internal").ap()
    io.linv_dr = nc.dram_tensor("linv_dr", [8, P], F32, kind="Internal").ap()
    io.ctxT = nc.dram_tensor("ctxTs", [D, T], BF16, kind="Internal").ap()
    return io


ALL_PHASES = ("setup", "ln1", "ffn1", "ln2", "qkv", "attn", "outproj", "ln3",
              "copyout", "ffn2")


def build_program(phases=ALL_PHASES, debug_out=()):
    nc = bass.Bass("TRN2", target_bir_lowering=False)
    io = build_io(nc)

    with contextlib.ExitStack() as stack:
        p = Prog(nc, stack)
        for _gn in ("setup_dma_g", "at_swv", "at_soa0", "at_soa1", "at_swo",
                    "ffn1_sw1", "ffn1_sw2", "ffn1_sb2", "ffn1_sac0", "ffn1_sac1",
                    "ffn2_sw1", "ffn2_sw2", "ffn2_sb2", "ffn2_sac0", "ffn2_sac1",
                    "ffn1_sga0", "ffn1_sga1", "ffn2_sga0", "ffn2_sga1",
                    "ffn1_sbr", "ffn2_sbr",
                    "ffn1_slr0", "ffn1_slr1", "ffn1_slr2", "ffn1_slr3",
                    "ffn2_slr0", "ffn2_slr1", "ffn2_slr2", "ffn2_slr3",
                    "ln1_slg0", "ln1_slg1", "ln3_slg0", "ln3_slg1",
                    "ln1_swc0", "ln1_swc1"):
            p.gsem(_gn)
        finals = []
        if "setup" in phases:
            emit_setup(p, io)
        if "ln1" in phases:
            emit_ln(p, io, 1, io.x_in, with_gate=True, precast=True)
        if "ffn1" in phases:
            emit_ffn_sparse(p, io, 1, io.xwork)
        if "ln2" in phases:
            emit_ln(p, io, 2, io.xwork, with_gate=False)
        if "attn" in phases:
            emit_attention(p, io, wait_lnT=None)
        if "ln3" in phases:
            emit_ln(p, io, 3, io.xwork, with_gate=True)
        if "copyout" in phases:
            with p.phase():
                sco = p.sem("s_copyout")
                p.clear_block("copyout")
                with nc.Block("copyout") as b:
                    @b.sync
                    def _(s):
                        s.dma_start(io.y[:, :], io.xwork[:, :]).then_inc(sco, 16)
                        s.wait_ge(sco, 16)
        if "ffn2" in phases:
            emit_ffn_sparse(p, io, 2, io.y)

        if debug_out:
            with p.phase():
                sdbg = p.sem("s_dbg")
                p.clear_block("dbg")
                with nc.Block("dbg") as b:
                    @b.sync
                    def _(s):
                        m = 0
                        for name in debug_out:
                            srcap = getattr(io, name)
                            dbg = nc.dram_tensor(f"dbg_{name}", list(srcap.shape),
                                                 srcap.tensor.dtype,
                                                 kind="ExternalOutput").ap()
                            s.dma_start(dbg, srcap).then_inc(sdbg, 16)
                            m += 16
                        s.wait_ge(sdbg, m)

        with nc.Block("final") as b:
            @b.sync
            def _(s):
                s.nop()
    return nc


def make_in_map(inputs, core):
    return dict(
        x=np.ascontiguousarray(inputs["L0_attn_x"][core]),
        Wg=inputs["L0_ffn_Wg"], bg=inputs["L0_ffn_bg"],
        W1=inputs["L0_ffn_W1"], b1=inputs["L0_ffn_b1"],
        W2=inputs["L0_ffn_W2"], b2=inputs["L0_ffn_b2"],
        Wq=inputs["L0_attn_Wq"], bq=inputs["L0_attn_bq"],
        Wk=inputs["L0_attn_Wk"], bk=inputs["L0_attn_bk"],
        Wv=inputs["L0_attn_Wv"], bv=inputs["L0_attn_bv"],
        Wo=inputs["L0_attn_Wo"], bo=inputs["L0_attn_bo"],
    )


def kernel(**inputs):
    inputs = {k: np.asarray(v, dtype=np.float32) for k, v in inputs.items()}
    nc = build_program(ALL_PHASES)
    in_maps = [make_in_map(inputs, c) for c in range(8)]
    res = run_bass_kernel_spmd(nc, in_maps, core_ids=list(range(8)))
    return np.stack([res.results[c]["y"] for c in range(8)], axis=0)


# version 13 (final)
# speedup vs baseline: 1.0754x; 1.0754x over previous
import contextlib
import numpy as np

import concourse.bass as bass
import concourse.mybir as mybir
from concourse.bass_utils import run_bass_kernel_spmd

F32 = mybir.dt.float32
BF16 = mybir.dt.bfloat16
I32 = mybir.dt.int32
AF = mybir.ActivationFunctionType
ALU = mybir.AluOpType
AX = mybir.AxisListType

P = 128
T = 4096
D = 1024
HID = 4096
E = 8
NH = 16
HD = 64
TT = T // P
DT = D // P
HT = HID // P
CH = 8
CT = T // CH
CTT = CT // P
NCH = 16
EPS = 1e-5

_CHAINED = {
    "tensor_copy", "tensor_tensor", "tensor_add", "tensor_sub", "tensor_mul",
    "tensor_max", "tensor_scalar", "tensor_single_scalar", "tensor_scalar_add",
    "tensor_scalar_sub", "tensor_scalar_mul", "tensor_scalar_max", "tensor_relu",
    "tensor_reduce", "reduce_max", "reduce_sum", "scalar_tensor_tensor",
    "reciprocal", "memset", "memzero", "select", "copy_predicated", "iota",
    "affine_select", "activation", "square", "sqrt", "mul", "copy", "add", "sign",
}


_RISKY = {"tensor_reduce", "reduce_max", "reduce_sum", "reciprocal",
          "bn_stats", "bn_aggr"}
import os as _os
STRICT_CHAIN = _os.environ.get("STRICT_CHAIN", "vch")


class Chained:

    def __init__(self, eng, sem, name=""):
        self.eng = eng
        self.sem = sem
        self.name = name
        self.strict = ("all" in STRICT_CHAIN) or any(
            t and t in name for t in STRICT_CHAIN.split(","))
        self.n = 0
        self.pend = None

    def drain(self):
        if self.pend:
            self.eng.wait_ge(*self.pend)
            self.pend = None

    def __getattr__(self, name):
        eng = self.eng
        f = getattr(eng, name)
        if name not in _CHAINED:
            return f
        base_risky = self.strict or name in _RISKY

        def wrapped(*a, inc=None, **kw):
            risky = base_risky or kw.get("accum_out") is not None
            if self.pend:
                eng.wait_ge(*self.pend)
                self.pend = None
            inst = f(*a, **kw)
            if inc is not None:
                sem, val = inc
                inst.then_inc(sem, 1)
                if risky:
                    self.pend = (sem, val)
            elif risky:
                self.n += 1
                inst.then_inc(self.sem, 1)
                self.pend = (self.sem, self.n)
            return inst

        return wrapped


class Prog:
    def __init__(self, nc, stack):
        self.nc = nc
        self.stack = stack
        self.pstack = None
        self.sems = {}
        self.phase_sems = []

    def sem(self, name):
        if name not in self.sems:
            st = self.pstack if self.pstack is not None else self.stack
            h = st.enter_context(self.nc.semaphore(name))
            self.sems[name] = h
            if self.pstack is not None:
                self.phase_sems.append(h)
        return self.sems[name]

    def gsem(self, name):
        if name not in self.sems:
            h = self.stack.enter_context(self.nc.semaphore(name))
            self.sems[name] = h
        return self.sems[name]

    def clear_block(self, tag):
        sems = list(self.phase_sems)
        with self.nc.Block(f"{tag}_clr") as b:
            @b.sync
            def _(s):
                for h in sems:
                    s.sem_clear(h)

    def chain(self, eng, name):
        return Chained(eng, self.sem(name), name)

    def sbuf_g(self, name, shape, dtype):
        return self.stack.enter_context(self.nc.sbuf_tensor(name, shape, dtype))

    def sbuf(self, name, shape, dtype):
        return self.pstack.enter_context(self.nc.sbuf_tensor(name, shape, dtype))

    def psum(self, name, shape, dtype=F32):
        return self.pstack.enter_context(self.nc.psum_tensor(name, shape, dtype))

    @contextlib.contextmanager
    def phase(self):
        with contextlib.ExitStack() as ps:
            self.pstack = ps
            self.phase_sems = []
            yield
            self.pstack = None
            for h in self.phase_sems:
                self.sems = {k: v for k, v in self.sems.items() if v is not h}
            self.phase_sems = []


def emit_setup(p, io):
    nc = p.nc
    p.ident_bf = p.sbuf_g("ident_bf", [P, P], BF16)
    p.ident_f32 = p.sbuf_g("ident_f32", [P, P], F32)
    p.ones3 = p.sbuf_g("ones3", [3, P], BF16)
    p.ones_bf = p.sbuf_g("ones_bf", [P, P], BF16)
    p.mneg1 = p.sbuf_g("mneg1_bf", [P, P], BF16)
    p.mnegT = p.sbuf_g("mnegT_bf", [P, P], BF16)
    p.striu_bf = p.sbuf_g("striu_bf", [P, P], BF16)
    p.wg_sb = p.sbuf_g("wg_sb", [P, DT * E], BF16)
    p.bgB = p.sbuf_g("bgB", [P, E], F32)
    p.slots = p.sbuf_g("slots_i32", [P, TT], mybir.dt.int32)
    p.halfw = p.sbuf_g("halfw_sb", [P, TT], F32)
    p.eps_col = p.sbuf_g("eps_col", [P, 1], F32)
    p.nc20 = p.sbuf_g("nc20_col", [P, 1], F32)
    p.MSK32 = p.sbuf_g("msk32", [P, TT * E], F32)
    p.RK32 = p.sbuf_g("rk32", [P, TT], F32)
    p.IDXW = p.sbuf_g("idxw_i32", [P, NCH], I32)
    p.IDXB1 = p.sbuf_g("idxb1_i32", [P, NCH], I32)
    p.IDXE = p.sbuf_g("idxe_i32", [P, NCH], I32)
    p.pcolf = p.sbuf_g("pcolf", [P, 1], F32)
    p.CCrowF = p.sbuf_g("ccrowf", [P, NCH], F32)
    p.TK32 = p.sbuf_g("tk32_i32", [P, TT], I32)

    sd = p.sem("setup_dma")
    sdg = p.gsem("setup_dma_g")
    with p.phase():
        scratch = p.sbuf("setup_scratch", [P, P], F32)
        with nc.Block("setup") as b:
            @b.sync
            def _(s):
                s.dma_start(io.xwork[:, :], io.x_in[:, :]).then_inc(sd, 16)
                s.dma_start(p.bgB[:, :],
                            bass.AP(io.bg.tensor, 0, [[0, P], [1, E]])).then_inc(sd, 16)
                s.wait_ge(sd, 32)
                s.wait_ge(sdg, 16)

            @b.gpsimd
            def _(g0):
                g0.dma_start(p.wg_sb[:, :].rearrange("p (dt e) -> p dt e", dt=DT),
                             io.wg.rearrange("(dt p) e -> p dt e", p=P)).then_inc(sdg, 16)
                g = p.chain(g0, "setup_gch")
                g.memset(p.ones3[:, :], 1.0)
                g.memset(p.eps_col[:, :], EPS)
                g.memset(p.nc20[:, :], -20.0)
                g.memset(p.ones_bf[:, :], 1.0)
                g.iota(p.pcolf[:, :], [[1, 1]], channel_multiplier=1,
                       allow_small_or_imprecise_dtypes=True)
                g.iota(p.CCrowF[:, :], [[1, NCH]], channel_multiplier=0,
                       allow_small_or_imprecise_dtypes=True)
                g.iota(p.TK32[:, :], [[P, TT]], channel_multiplier=1,
                       allow_small_or_imprecise_dtypes=True)
                g.memset(scratch[:, :], 1.0)
                g.affine_select(p.ident_bf[:, :], scratch[:, :], [[1, P]], ALU.is_equal,
                                0.0, channel_multiplier=-1)
                g.affine_select(p.ident_f32[:, :], scratch[:, :], [[1, P]], ALU.is_equal,
                                0.0, channel_multiplier=-1)
                g.affine_select(p.striu_bf[:, :], p.ones_bf[:, :], [[1, P]], ALU.is_ge,
                                0.0, base=-1, channel_multiplier=-1)
                g.memset(scratch[:, :], -10000.0)
                g.affine_select(p.mneg1[:, :], scratch[:, :], [[1, P]], ALU.is_ge,
                                0.0, base=-1, channel_multiplier=-1)
                g.affine_select(p.mnegT[:, :], scratch[:, :], [[-1, P]], ALU.is_ge,
                                0.0, base=-1, channel_multiplier=1)
                g.drain()
                g0.wait_ge(sd, 32)
                g0.wait_ge(sdg, 16)


def emit_ln(p, io, idx, src, with_gate, precast=False):
    nc = p.nc
    tag = f"ln{idx}"
    p.phase_ctx = p.phase()
    p.phase_ctx.__enter__()
    sx = [p.sem(f"{tag}_sx{i}") for i in range(2)]
    ssq = p.sem(f"{tag}_ssq")
    svar = p.sem(f"{tag}_svar")
    ssd = p.sem(f"{tag}_ssd")
    srec = p.sem(f"{tag}_srec")
    sln = p.sem(f"{tag}_sln")
    smm8 = p.sem(f"{tag}_smm8")
    scp8 = p.sem(f"{tag}_scp8")
    slt = [p.sem(f"{tag}_slt{i}") for i in range(2)]
    if with_gate:
        sgm = p.sem(f"{tag}_sgm")
        sgv = p.sem(f"{tag}_sgv")
        sge = p.sem(f"{tag}_sge")
        sgd = p.sem(f"{tag}_sgd")
        smk = p.sem(f"{tag}_smk")
        srk = p.sem(f"{tag}_srk")
        ssl = p.sem(f"{tag}_ssl")
        ssl2 = p.sem(f"{tag}_ssl2")
        sld8 = p.sem(f"{tag}_sld8")
        svi = p.sem(f"{tag}_svi")
        stin = p.sem(f"{tag}_stin")
        slg = [p.gsem(f"{tag}_slg{i}") for i in range(2)]
    if precast:
        swc = [p.gsem(f"{tag}_swc{i}") for i in range(2)]
        swcs = [p.sem(f"{tag}_swcs{i}") for i in range(2)]
    p.sem(f"{tag}_sch")
    p.sem(f"{tag}_vch")
    p.clear_block(tag)
    if True:
        xb = [p.sbuf(f"{tag}_x{i}", [P, D], F32) for i in range(2)]
        sq = [p.sbuf(f"{tag}_sq{i}", [P, D], F32) for i in range(2)]
        lnb = [p.sbuf(f"{tag}_ln{i}", [P, D], BF16) for i in range(2)]
        stag = [p.sbuf(f"{tag}_st{i}", [P, D], BF16) for i in range(2)]
        cols = {n: p.sbuf(f"{tag}_{n}", [P, 1], F32)
                for n in ["s1", "s2", "mean", "var", "sd", "rstd", "negmb", "m8", "nm8",
                          "S", "w", "t8"]}
        g8 = p.sbuf(f"{tag}_g8", [P, E], F32)
        sgn8 = p.sbuf(f"{tag}_sgn8", [P, E], F32)
        p8 = p.sbuf(f"{tag}_p8", [P, E], F32)
        gt8 = p.sbuf(f"{tag}_gt8", [P, E], F32)
        trp = [p.psum(f"{tag}_trp{i}", [P, P], BF16) for i in range(2)]
        lgp = [p.psum(f"{tag}_lgp{i}", [P, E], F32) for i in range(2)] if with_gate else None
        if with_gate:
            maskbf = p.sbuf(f"{tag}_maskbf", [P, TT * E], BF16)
            rkp = [p.psum(f"{tag}_rkp{i}", [P, E], F32) for i in range(2)]
            slotf = p.sbuf(f"{tag}_slotf", [P, 1], F32)
            r8 = p.sbuf(f"{tag}_r8", [P, E], F32)
            l8 = p.sbuf(f"{tag}_l8", [P, E], F32)
            nch8 = p.sbuf(f"{tag}_nch8", [P, E], F32)
            cpa = p.sbuf(f"{tag}_cpa", [P, E], F32)
            cpb = p.sbuf(f"{tag}_cpb", [P, E], F32)
            padoff8 = p.sbuf(f"{tag}_padoff8", [P, E], F32)
            ecf = p.sbuf(f"{tag}_ecf", [P, NCH], F32)
            ect = p.sbuf(f"{tag}_ect", [P, NCH], F32)
            mtf = p.sbuf(f"{tag}_mtf", [P, NCH * CT // P], F32)
            mti = p.sbuf(f"{tag}_mti", [P, NCH * CT // P], I32)
        if precast:
            wcs = [p.sbuf(f"{tag}_wc{i}", [P, 8192], BF16) for i in range(2)]
            witems = [("w1", e, dt) for e in range(E) for dt in range(DT)] + \
                     [("w2", e, g4) for e in range(E) for g4 in range(4)]

        with nc.Block(tag) as b:
            if with_gate:
                @b.gpsimd
                def _(g):
                    g.wait_ge(svi, 1)
                    g.dma_start(bass.AP(io.tokof.tensor, 0,
                                        [[NCH * CT // P, P], [1, NCH * CT // P]]),
                                mti[:, :]).then_inc(stin, 16)
                    if precast:
                        for i, (kind, e, j) in enumerate(witems):
                            if i >= 2:
                                g.wait_ge(swcs[i % 2], 16 * (i // 2))
                            if kind == "w1":
                                g.dma_start(
                                    wcs[i % 2][:, 0:HID],
                                    io.w1[e][j * P:(j + 1) * P, :],
                                ).then_inc(swc[i % 2], 16)
                            else:
                                g.dma_start(
                                    wcs[i % 2][:, :].rearrange(
                                        "p (g d) -> p g d", g=8),
                                    io.w2[e][j * 1024:(j + 1) * 1024, :].rearrange(
                                        "(g p) d -> p g d", p=P),
                                ).then_inc(swc[i % 2], 16)
                    g.wait_ge(stin, 16)
                    for tt in range(TT):
                        g.wait_ge(ssl2, tt + 1)
                        g.indirect_dma_start(
                            io.tokof,
                            bass.IndirectOffsetOnAxis(p.slots[:, tt:tt + 1], 0),
                            p.TK32[:, tt:tt + 1], None).then_inc(slg[tt % 2], 16)
                    g.wait_ge(slg[0], 16 * (TT // 2))
                    g.wait_ge(slg[1], 16 * (TT // 2))

            @b.sync
            def _(s):
                if precast:
                    wstore = [0]

                    def emit_wstore():
                        i = wstore[0]
                        if i >= len(witems):
                            return
                        wstore[0] += 1
                        kind, e, j = witems[i]
                        s.wait_ge(swc[i % 2], 16 * (i // 2 + 1))
                        if kind == "w1":
                            s.dma_start(
                                io.W1bf[e * P:(e + 1) * P,
                                        j * HID:(j + 1) * HID],
                                wcs[i % 2][:, 0:HID]).then_inc(swcs[i % 2], 16)
                        else:
                            s.dma_start(
                                io.W2bf[e * P:(e + 1) * P,
                                        j * 8192:(j + 1) * 8192],
                                wcs[i % 2][:, :]).then_inc(swcs[i % 2], 16)

                for tt in range(TT):
                    if tt >= 2:
                        s.wait_ge(sln, tt - 1)
                        s.wait_ge(svar, tt - 1)
                    s.dma_start(xb[tt % 2][:, :],
                                src[tt * P:(tt + 1) * P, :]).then_inc(sx[tt % 2], 16)
                    if with_gate:
                        s.wait_ge(sln, tt + 1)
                        s.dma_start(io.lnR[tt * P:(tt + 1) * P, :],
                                    lnb[tt % 2][:, :]).then_inc(slt[tt % 2], 16)
                    else:
                        s.wait_ge(scp8, (tt + 1) * DT)
                        s.dma_start(
                            io.lnT.rearrange("(dt p) t -> p dt t", p=P)[:, :, tt * P:(tt + 1) * P],
                            stag[tt % 2][:, :].rearrange("p (dt c) -> p dt c", dt=DT),
                        ).then_inc(slt[tt % 2], 16)
                    if precast:
                        for _ in range(3):
                            emit_wstore()
                if precast:
                    while wstore[0] < len(witems):
                        emit_wstore()
                    s.wait_ge(swcs[0], 16 * (len(witems) // 2))
                    s.wait_ge(swcs[1], 16 * (len(witems) // 2))
                s.wait_ge(slt[0], 16 * (TT // 2))
                s.wait_ge(slt[1], 16 * (TT // 2))

            @b.scalar
            def _(s0):
                s = p.chain(s0, f"{tag}_sch")
                for tt in range(TT):
                    s0.wait_ge(sx[tt % 2], 16 * (tt // 2 + 1))
                    if tt >= 2:
                        s0.wait_ge(svar, tt - 1)
                    s.square(sq[tt % 2][:, :], xb[tt % 2][:, :], inc=(ssq, tt + 1))
                    s0.wait_ge(svar, tt + 1)
                    if tt >= 1:
                        s0.wait_ge(srec, tt)
                    s.activation(cols["sd"][:, :], cols["var"][:, :], AF.Sqrt,
                                 bias=p.eps_col[:, :], inc=(ssd, tt + 1))
                    s0.wait_ge(srec, tt + 1)
                    if tt >= 2:
                        s0.wait_ge(scp8, (tt - 1) * DT)
                        if with_gate:
                            s0.wait_ge(slt[tt % 2], 16 * (tt // 2))
                    s.activation(lnb[tt % 2][:, :], xb[tt % 2][:, :], AF.Identity,
                                 bias=cols["negmb"][:, :],
                                 scale=cols["rstd"][:, :], inc=(sln, tt + 1))
                    if with_gate:
                        s0.wait_ge(sgv, tt + 1)
                        if tt >= 1:
                            s0.wait_ge(sgd, tt)
                        s.activation(p8[:, :], g8[:, :], AF.Exp, bias=cols["nm8"][:, :],
                                     accum_out=cols["S"][:, :])
                        s.activation(sgn8[:, :], g8[:, :], AF.Sign,
                                     bias=cols["nm8"][:, :], inc=(sge, tt + 1))

            @b.vector
            def _(v0):
                v = p.chain(v0, f"{tag}_vch")
                if with_gate:
                    v.memset(mtf[:, :], float(T - 1))
                    v.tensor_copy(mti[:, :], mtf[:, :], inc=(svi, 1))
                for tt in range(TT):
                    v0.wait_ge(sx[tt % 2], 16 * (tt // 2 + 1))
                    v.reduce_sum(cols["s1"][:, :], xb[tt % 2][:, :], AX.X)
                    v0.wait_ge(ssq, tt + 1)
                    v.reduce_sum(cols["s2"][:, :], sq[tt % 2][:, :], AX.X)
                    v.tensor_scalar(cols["mean"][:, :], cols["s1"][:, :], 1.0 / D, None,
                                    ALU.mult)
                    v.tensor_scalar(cols["var"][:, :], cols["s2"][:, :], 1.0 / D, None,
                                    ALU.mult)
                    v.tensor_tensor(cols["s1"][:, :], cols["mean"][:, :],
                                    cols["mean"][:, :], ALU.mult)
                    v.tensor_tensor(cols["var"][:, :], cols["var"][:, :], cols["s1"][:, :],
                                    ALU.subtract, inc=(svar, tt + 1))
                    v0.wait_ge(ssd, tt + 1)
                    if tt >= 1:
                        v0.wait_ge(sln, tt)
                    v.reciprocal(cols["rstd"][:, :], cols["sd"][:, :])
                    v.tensor_scalar(cols["negmb"][:, :], cols["mean"][:, :],
                                    cols["rstd"][:, :], -1.0, ALU.mult,
                                    ALU.mult, inc=(srec, tt + 1))
                    if tt >= 2:
                        if with_gate:
                            v0.wait_ge(sgm, tt - 1)
                        else:
                            v0.wait_ge(slt[tt % 2], 16 * (tt // 2))
                    for dt in range(DT):
                        n = tt * DT + dt
                        v0.wait_ge(smm8, n + 1)
                        v.tensor_copy(stag[tt % 2][:, dt * P:(dt + 1) * P],
                                      trp[dt % 2][:, :], inc=(scp8, n + 1))
                    if with_gate:
                        v0.wait_ge(sgm, tt + 1)
                        if tt >= 1:
                            v0.wait_ge(sge, tt)
                        v.tensor_tensor(g8[:, :], lgp[tt % 2][:, :], p.bgB[:, :], ALU.add)
                        v.reduce_max(cols["m8"][:, :], g8[:, :], AX.X)
                        v.reduce_max(cols["nm8"][:, :], g8[:, :], AX.X,
                                     negate=True, inc=(sgv, tt + 1))
                        v0.wait_ge(sge, tt + 1)
                        v.reciprocal(cols["w"][:, :], cols["S"][:, :])
                        v.tensor_scalar(cols["t8"][:, :], cols["w"][:, :], 0.5, None,
                                        ALU.mult)
                        v.tensor_copy(p.halfw[:, tt:tt + 1], cols["t8"][:, :])
                        v.tensor_scalar(gt8[:, :], sgn8[:, :], 1.0, None, ALU.add)
                        v.tensor_copy(maskbf[:, tt * E:(tt + 1) * E], gt8[:, :],
                                      inc=(smk, tt + 1))
                        v.tensor_copy(p.MSK32[:, tt * E:(tt + 1) * E], gt8[:, :])
                        v.tensor_scalar(gt8[:, :], gt8[:, :], 1.0, None,
                                        ALU.mult, inc=(sgd, tt + 1))
                        v0.wait_ge(srk, tt + 1)
                        v.tensor_tensor(r8[:, :], rkp[tt % 2][:, :], gt8[:, :], ALU.mult)
                        v.reduce_sum(slotf[:, :], r8[:, :], AX.X)
                        v.tensor_copy(p.RK32[:, tt:tt + 1], slotf[:, :],
                                      inc=(ssl, tt + 1))
                if with_gate:
                    v0.wait_ge(sld8, 1)
                    v.tensor_copy(l8[:, :], lgp[0][:, :])
                    v.memset(nch8[:, :], 0.0)
                    for kk in range(NCH):
                        v.tensor_scalar(cpa[:, :], l8[:, :], -float(kk * CT),
                                        None, ALU.add)
                        v.tensor_scalar(cpa[:, :], cpa[:, :], 0.0, 1.0,
                                        ALU.max, ALU.min)
                        v.tensor_tensor(nch8[:, :], nch8[:, :], cpa[:, :], ALU.add)
                    v.tensor_copy(cpa[:, :], nch8[:, :])
                    v.tensor_copy(cpb[:, :], cpa[:, :])
                    v.tensor_tensor(cpb[:, 1:E], cpa[:, 1:E],
                                    cpa[:, 0:E - 1], ALU.add)
                    v.tensor_copy(cpa[:, :], cpb[:, :])
                    v.tensor_tensor(cpa[:, 2:E], cpb[:, 2:E],
                                    cpb[:, 0:E - 2], ALU.add)
                    v.tensor_copy(cpb[:, :], cpa[:, :])
                    v.tensor_tensor(cpb[:, 4:E], cpa[:, 4:E],
                                    cpa[:, 0:E - 4], ALU.add)
                    v.tensor_tensor(cpb[:, :], cpb[:, :], nch8[:, :], ALU.subtract)
                    v.tensor_scalar(padoff8[:, :], cpb[:, :], float(CT), None,
                                    ALU.mult)
                    v.memset(ecf[:, :], -1.0)
                    for e in range(E):
                        v.tensor_scalar(ect[:, :], p.CCrowF[:, :],
                                        cpb[:, e:e + 1], 1.0,
                                        ALU.subtract, ALU.add)
                        v.tensor_scalar(ect[:, :], ect[:, :], 0.0, 1.0,
                                        ALU.max, ALU.min)
                        v.tensor_tensor(ecf[:, :], ecf[:, :], ect[:, :], ALU.add)
                    v.tensor_copy(p.IDXE[:, :], ecf[:, :])
                    v.tensor_scalar(ect[:, :], ecf[:, :], 32.0, None, ALU.mult)
                    v.tensor_scalar(ect[:, :], ect[:, :], p.pcolf[:, 0:1],
                                    None, ALU.add)
                    v.tensor_copy(p.IDXB1[:, :], ect[:, :])
                    v.tensor_scalar(ect[:, :], ecf[:, :], 128.0, None, ALU.mult)
                    v.tensor_scalar(ect[:, :], ect[:, :], p.pcolf[:, 0:1],
                                    None, ALU.add)
                    v.tensor_copy(p.IDXW[:, :], ect[:, :])
                    for tt in range(TT):
                        v.tensor_tensor(r8[:, :], p.MSK32[:, tt * E:(tt + 1) * E],
                                        padoff8[:, :], ALU.mult)
                        v.reduce_sum(slotf[:, :], r8[:, :], AX.X)
                        v.tensor_tensor(slotf[:, :], slotf[:, :],
                                        p.RK32[:, tt:tt + 1], ALU.add)
                        v.tensor_scalar(slotf[:, :], slotf[:, :],
                                        float(NCH * CT - 1), 0.0, ALU.min, ALU.max)
                        v.tensor_copy(p.slots[:, tt:tt + 1], slotf[:, :],
                                      inc=(ssl2, tt + 1))

            @b.tensor
            def _(t):
                for tt in range(TT):
                    t.wait_ge(sln, tt + 1)
                    for dt in range(DT):
                        n = tt * DT + dt
                        if n >= 2:
                            t.wait_ge(scp8, n - 1)
                        t.transpose(trp[dt % 2][:, :],
                                    lnb[tt % 2][:, dt * P:(dt + 1) * P],
                                    p.ident_bf[:, :]).then_inc(smm8, 1)
                    if with_gate:
                        t.wait_ge(scp8, (tt + 1) * DT)
                        if tt >= 2:
                            t.wait_ge(sgv, tt - 1)
                        inst = None
                        for dt in range(DT):
                            inst = t.matmul(lgp[tt % 2][:, :],
                                            stag[tt % 2][:, dt * P:(dt + 1) * P],
                                            p.wg_sb[:, dt * E:(dt + 1) * E],
                                            start=(dt == 0), stop=(dt == DT - 1))
                        inst.then_inc(sgm, 1)
                        t.wait_ge(smk, tt + 1)
                        if tt >= 2:
                            t.wait_ge(ssl, tt - 1)
                        inst = None
                        for t2 in range(tt + 1):
                            inst = t.matmul(rkp[tt % 2][:, :],
                                            p.ones_bf[:, :] if t2 < tt else p.striu_bf[:, :],
                                            maskbf[:, t2 * E:(t2 + 1) * E],
                                            start=(t2 == 0), stop=(t2 == tt))
                        inst.then_inc(srk, 1)
                if with_gate:
                    t.wait_ge(sgv, TT)
                    t.wait_ge(smk, TT)
                    inst = None
                    for t2 in range(TT):
                        inst = t.matmul(lgp[0][:, :], p.ones_bf[:, :],
                                        maskbf[:, t2 * E:(t2 + 1) * E],
                                        start=(t2 == 0), stop=(t2 == TT - 1))
                    inst.then_inc(sld8, 1)
    p.phase_ctx.__exit__(None, None, None)


def emit_ffn(p, io, idx, acc, wait_lnT):
    nc = p.nc
    tag = f"ffn{idx}"
    p.phase_ctx2 = p.phase()
    p.phase_ctx2.__enter__()
    slc = [p.sem(f"{tag}_slc{i}") for i in range(2)]
    sw1 = p.gsem(f"{tag}_sw1")
    sw2 = p.gsem(f"{tag}_sw2")
    sb2 = p.gsem(f"{tag}_sb2")
    sb1 = p.sem(f"{tag}_sb1")
    stb1 = p.sem(f"{tag}_stb1")
    scb1 = p.sem(f"{tag}_scb1")
    sha = p.sem(f"{tag}_sha")
    shs = p.sem(f"{tag}_shs")
    sbm = p.sem(f"{tag}_sbm")
    sye = p.sem(f"{tag}_sye")
    sac = [p.gsem(f"{tag}_sac{i}") for i in range(2)]
    p.sem(f"{tag}_sch")
    p.sem(f"{tag}_vch")


    def amm(e, c, ht):
        return (e * CH + c) * HT + ht + 1

    def bmm(e, c, tt):
        return (e * CH + c) * CTT + tt + 1

    p.clear_block(tag)
    if True:
        w1 = p.sbuf(f"{tag}_w1", [P, DT * HID], BF16)
        w2 = p.sbuf(f"{tag}_w2", [P, HT * D], BF16)
        hT = p.sbuf(f"{tag}_hT", [P, HT * CT], BF16)
        lnc = [p.sbuf(f"{tag}_lnc{i}", [P, DT * CT], BF16) for i in range(2)]
        b2B = p.sbuf(f"{tag}_b2B", [P, D], F32)
        b1row = p.sbuf(f"{tag}_b1row", [HT, P], F32)
        b1col = p.sbuf(f"{tag}_b1col", [P, HT], F32)
        yst = [p.sbuf(f"{tag}_yst{i}", [P, D], F32) for i in range(2)]
        psa = [p.psum(f"{tag}_psa{i}", [P, CT], F32) for i in range(2)]
        psy = [p.psum(f"{tag}_psy{i}", [P, D], F32) for i in range(2)]
        psb1 = p.psum(f"{tag}_psb1", [P, HT], F32)

        with nc.Block(tag) as b:
            @b.gpsimd
            def _(g):
                def load_w1(e):
                    for q in range(4):
                        g.dma_start(
                            w1[:, :].rearrange("p (dt h) -> p dt h", dt=DT)
                            [:, :, q * 1024:(q + 1) * 1024],
                            io.w1[e].rearrange("(dt p) h -> p dt h", p=P)
                            [:, :, q * 1024:(q + 1) * 1024],
                        ).then_inc(sw1, 16)

                def load_w2(e):
                    for q in range(4):
                        g.dma_start(
                            w2[:, :].rearrange("p (ht d) -> p ht d", ht=HT)
                            [:, 8 * q:8 * (q + 1), :],
                            io.w2[e].rearrange("(ht p) d -> p ht d", p=P)
                            [:, 8 * q:8 * (q + 1), :],
                        ).then_inc(sw2, 16)

                load_w1(0)
                load_w2(0)
                for e in range(E):
                    if e > 0:
                        g.wait_ge(sye, bmm(e - 1, CH - 1, CTT - 1))
                        g.wait_ge(sac[0], 16 * (TT // 2) * e)
                        g.wait_ge(sac[1], 16 * (TT // 2) * e)
                    g.dma_start(b2B[:, :],
                                bass.AP(io.b2.tensor, e * D, [[0, P], [1, D]])
                                ).then_inc(sb2, 16)
                    for c in range(CH):
                        for tt in range(CTT):
                            gt = c * CTT + tt
                            g.wait_ge(sye, bmm(e, c, tt))
                            g.dma_start(acc[gt * P:(gt + 1) * P, :], yst[gt % 2][:, :],
                                        accum_op=ALU.add).then_inc(sac[gt % 2], 16)
                        if c == CH - 2 and e < E - 1:
                            g.wait_ge(sha, amm(e, CH - 1, HT - 1))
                            load_w1(e + 1)
                    if e < E - 1:
                        g.wait_ge(sbm, bmm(e, CH - 1, CTT - 1))
                        load_w2(e + 1)
                g.wait_ge(sac[0], 16 * (TT // 2) * E)
                g.wait_ge(sac[1], 16 * (TT // 2) * E)

            @b.sync
            def _(s):
                for e in range(E):
                    if e > 0:
                        s.wait_ge(stb1, e)
                    s.dma_start(b1row[:, :],
                                io.b1[e * HT:(e + 1) * HT, :]).then_inc(sb1, 16)
                    for c in range(CH):
                        k = e * CH + c
                        if k >= 2:
                            s.wait_ge(sha, amm((k - 2) // CH, (k - 2) % CH, HT - 1))
                        s.dma_start(
                            lnc[k % 2][:, :].rearrange("p (dt t) -> p dt t", dt=DT),
                            io.lnT.rearrange("(dt p) t -> p dt t", p=P)
                            [:, :, c * CT:(c + 1) * CT],
                        ).then_inc(slc[k % 2], 16)

            @b.tensor
            def _(t):
                for e in range(E):
                    t.wait_ge(sb1, 16 * (e + 1))
                    if e > 0:
                        t.wait_ge(scb1, e)
                    t.transpose(psb1[:, :], b1row[:, :],
                                p.ident_f32[0:HT, 0:HT]).then_inc(stb1, 1)
                    for c in range(CH):
                        k = e * CH + c
                        for ht in range(HT):
                            if ht == 0:
                                t.wait_ge(slc[k % 2], 16 * (k // 2 + 1))
                                t.wait_ge(sw1, 16 * 4 * (e + 1))
                            pr = k * HT + ht - 2
                            if pr >= 0:
                                t.wait_ge(shs, pr + 1)
                            last = None
                            for dt in range(DT):
                                last = t.matmul(
                                    psa[ht % 2][:, :],
                                    w1[:, dt * HID + ht * P: dt * HID + (ht + 1) * P],
                                    lnc[k % 2][:, dt * CT:(dt + 1) * CT],
                                    start=(dt == 0), stop=(dt == DT - 1))
                            last.then_inc(sha, 1)
                        for tt in range(CTT):
                            if tt == 0:
                                t.wait_ge(shs, (k + 1) * HT)
                                if c == 0:
                                    t.wait_ge(sw2, 16 * 4 * (e + 1))
                            bno = bmm(e, c, tt)
                            if bno >= 3:
                                t.wait_ge(sye, bno - 2)
                            gt = c * CTT + tt
                            last = None
                            for ht in range(HT):
                                for half in range(2):
                                    last = t.matmul(
                                        psy[gt % 2][:, half * 512:(half + 1) * 512],
                                        hT[:, ht * CT + tt * P: ht * CT + (tt + 1) * P],
                                        w2[:, ht * D + half * 512: ht * D + (half + 1) * 512],
                                        start=(ht == 0), stop=(ht == HT - 1))
                            last.then_inc(sbm, 1)

            @b.scalar
            def _(s0):
                s = p.chain(s0, f"{tag}_sch")
                for e in range(E):
                    s0.wait_ge(scb1, e + 1)
                    for c in range(CH):
                        k = e * CH + c
                        for ht in range(HT):
                            s0.wait_ge(sha, amm(e, c, ht))
                            if ht == 0 and k > 0:
                                s0.wait_ge(sbm, k * CTT)
                            s.activation(hT[:, ht * CT:(ht + 1) * CT], psa[ht % 2][:, :],
                                         AF.Silu, bias=b1col[:, ht:ht + 1],
                                         inc=(shs, amm(e, c, ht)))

            @b.vector
            def _(v0):
                v = p.chain(v0, f"{tag}_vch")
                for e in range(E):
                    v0.wait_ge(stb1, e + 1)
                    if e > 0:
                        v0.wait_ge(shs, e * CH * HT)
                    v.tensor_copy(b1col[:, :], psb1[:, :], inc=(scb1, e + 1))
                    v0.wait_ge(sb2, 16 * (e + 1))
                    for c in range(CH):
                        for tt in range(CTT):
                            bno = bmm(e, c, tt)
                            v0.wait_ge(sbm, bno)
                            gt = c * CTT + tt
                            if bno >= 3:
                                v0.wait_ge(sac[gt % 2], 16 * ((bno - 2 + 1) // 2))
                            v.tensor_tensor(yst[gt % 2][:, :], psy[gt % 2][:, :],
                                            b2B[:, :], ALU.add)
                            v.tensor_scalar(yst[gt % 2][:, :], yst[gt % 2][:, :],
                                            p.s8[:, gt * E + e: gt * E + e + 1], None,
                                            ALU.mult, inc=(sye, bno))
    p.phase_ctx2.__exit__(None, None, None)


def emit_ffn_sparse(p, io, idx, acc):
    nc = p.nc
    tag = f"ffn{idx}"
    p.phase_ctx2 = p.phase()
    p.phase_ctx2.__enter__()
    sw1 = p.gsem(f"{tag}_sw1")
    sw2 = p.gsem(f"{tag}_sw2")
    sb2 = p.gsem(f"{tag}_sb2")
    sbr = p.gsem(f"{tag}_sbr")
    slr = [p.gsem(f"{tag}_slr{i}") for i in range(CTT)]
    sac = [p.gsem(f"{tag}_sac{i}") for i in range(2)]
    sga = [p.gsem(f"{tag}_sga{i}") for i in range(2)]
    stf = [p.sem(f"{tag}_stf{i}") for i in range(CTT)]
    sod = [p.sem(f"{tag}_sod{i}") for i in range(2)]
    stb1 = p.sem(f"{tag}_stb1")
    scb1 = p.sem(f"{tag}_scb1")
    sstr = p.sem(f"{tag}_sstr")
    scg = p.sem(f"{tag}_scg")
    sha = p.sem(f"{tag}_sha")
    shs = p.sem(f"{tag}_shs")
    sbm = p.sem(f"{tag}_sbm")
    soe = p.sem(f"{tag}_soe")
    syw = p.sem(f"{tag}_syw")
    p.sem(f"{tag}_sch")
    p.sem(f"{tag}_vch")

    def amm(k, ht):
        return k * HT + ht + 1

    def bmm(k, tl):
        return k * CTT + tl + 1

    p.clear_block(tag)
    if True:
        w1 = p.sbuf(f"{tag}_w1", [P, DT * HID], BF16)
        w2 = p.sbuf(f"{tag}_w2", [P, HT * D], BF16)
        hT = p.sbuf(f"{tag}_hT", [P, HT * CT], BF16)
        lnGst = p.sbuf(f"{tag}_lnGst", [P, CTT * D], BF16)
        lnGT = p.sbuf(f"{tag}_lnGT", [P, DT * CT], BF16)
        b2B = p.sbuf(f"{tag}_b2B", [P, D], F32)
        b1row = p.sbuf(f"{tag}_b1row", [HT, P], F32)
        b1col = p.sbuf(f"{tag}_b1col", [P, HT], F32)
        tkf = [p.sbuf(f"{tag}_tkf{i}", [P, CTT], I32) for i in range(2)]
        ost = [p.sbuf(f"{tag}_ost{i}", [P, D], F32) for i in range(2)]
        gst = [p.sbuf(f"{tag}_gst{i}", [P, D], F32) for i in range(2)]
        yst = ost
        psa = [p.psum(f"{tag}_psa{i}", [P, CT], F32) for i in range(2)]
        psy = [p.psum(f"{tag}_psy{i}", [P, D], F32) for i in range(2)]
        trp2 = [p.psum(f"{tag}_trp2{i}", [P, P], BF16) for i in range(2)]

        with nc.Block(tag) as b:
            @b.gpsimd
            def _(g):
                for k in range(NCH):
                    if k >= 1:
                        g.wait_ge(sstr, 32 * k)
                    for tl in range(CTT):
                        g.wait_ge(stf[tl], 16 * (k + 1))
                        g.indirect_dma_start(
                            lnGst[:, tl * D:(tl + 1) * D], None, io.lnR,
                            bass.IndirectOffsetOnAxis(tkf[k % 2][:, tl:tl + 1], 0)
                        ).then_inc(slr[tl], 16)
                    if k > 0:
                        g.wait_ge(stb1, k)
                    g.indirect_dma_start(
                        b1row[:, :], None, io.b1,
                        bass.IndirectOffsetOnAxis(p.IDXB1[0:HT, k:k + 1], 0)
                    ).then_inc(sbr, 16)
                    if k > 0:
                        g.wait_ge(sha, amm(k - 1, HT - 1))
                    g.indirect_dma_start(
                        w1[:, :], None, io.W1bf,
                        bass.IndirectOffsetOnAxis(p.IDXW[:, k:k + 1], 0)
                    ).then_inc(sw1, 16)
                    if k > 0:
                        g.wait_ge(soe, bmm(k - 1, CTT - 1))
                    g.indirect_dma_start(
                        b2B[:, :], None, io.b2,
                        bass.IndirectOffsetOnAxis(p.IDXE[:, k:k + 1], 0)
                    ).then_inc(sb2, 16)
                    if k > 0:
                        g.wait_ge(sbm, bmm(k - 1, CTT - 1))
                    g.indirect_dma_start(
                        w2[:, :], None, io.W2bf,
                        bass.IndirectOffsetOnAxis(p.IDXW[:, k:k + 1], 0)
                    ).then_inc(sw2, 16)
                g.wait_ge(sod[0], 16 * (NCH * CTT // 2))
                g.wait_ge(sod[1], 16 * (NCH * CTT // 2))
                for tt in range(TT):
                    if tt >= 2:
                        g.wait_ge(syw, tt - 1)
                    g.indirect_dma_start(
                        gst[tt % 2][:, :], None,
                        io.o_sp,
                        bass.IndirectOffsetOnAxis(p.slots[:, tt:tt + 1], 0),
                    ).then_inc(sga[tt % 2], 16)
                    if tt >= 1:
                        g.wait_ge(syw, tt)
                        g.dma_start(acc[(tt - 1) * P:tt * P, :], yst[(tt - 1) % 2][:, :],
                                    accum_op=ALU.add).then_inc(sac[(tt - 1) % 2], 16)
                g.wait_ge(syw, TT)
                g.dma_start(acc[(TT - 1) * P:TT * P, :], yst[(TT - 1) % 2][:, :],
                            accum_op=ALU.add).then_inc(sac[(TT - 1) % 2], 16)
                g.wait_ge(sac[0], 16 * (TT // 2))
                g.wait_ge(sac[1], 16 * (TT // 2))

            @b.sync
            def _(s):
                def tkf_load(k):
                    for tl in range(CTT):
                        if k >= 2:
                            s.wait_ge(slr[tl], 16 * (k - 1))
                        if k >= 1:
                            s.wait_ge(stf[tl], 16 * k)
                        s.dma_start(tkf[k % 2][:, tl:tl + 1],
                                    bass.AP(io.tokof.tensor, k * CT + tl * P,
                                            [[1, P], [1, 1]])
                                    ).then_inc(stf[tl], 16)

                tkf_load(0)
                for k in range(NCH):
                    if k + 1 < NCH:
                        tkf_load(k + 1)
                    for tl in range(CTT):
                        s.wait_ge(soe, bmm(k, tl))
                        s.dma_start(io.o_sp[(k * CTT + tl) * P:(k * CTT + tl + 1) * P, :],
                                    ost[tl % 2][:, :]).then_inc(sod[tl % 2], 16)

            @b.tensor
            def _(t):
                for k in range(NCH):
                    t.wait_ge(sbr, 16 * (k + 1))
                    if k > 0:
                        t.wait_ge(scb1, k)
                        t.wait_ge(shs, amm(k - 1, HT - 1))
                    t.transpose(psa[0][:, 0:HT], b1row[:, :],
                                p.ident_f32[0:HT, 0:HT]).then_inc(stb1, 1)
                    for tl in range(CTT):
                        t.wait_ge(slr[tl], 16 * (k + 1))
                    if k >= 1:
                        t.wait_ge(scg, 32 * k)
                    for n in range(HT):
                        tl, dt = divmod(n, DT)
                        if 32 * k + n >= 2:
                            t.wait_ge(scg, 32 * k + n - 1)
                        t.transpose(trp2[n % 2][:, :],
                                    lnGst[:, tl * D + dt * P: tl * D + (dt + 1) * P],
                                    p.ident_bf[:, :]).then_inc(sstr, 1)
                    for ht in range(HT):
                        if ht == 0:
                            t.wait_ge(scg, 32 * (k + 1))
                            t.wait_ge(sw1, 16 * (k + 1))
                        pr = k * HT + ht - 2
                        if pr >= 0:
                            t.wait_ge(shs, pr + 1)
                        last = None
                        for dt in range(DT):
                            last = t.matmul(
                                psa[ht % 2][:, :],
                                w1[:, dt * HID + ht * P: dt * HID + (ht + 1) * P],
                                lnGT[:, dt * CT:(dt + 1) * CT],
                                start=(dt == 0), stop=(dt == DT - 1))
                        last.then_inc(sha, 1)
                    for tl in range(CTT):
                        if tl == 0:
                            t.wait_ge(shs, (k + 1) * HT)
                            t.wait_ge(sw2, 16 * (k + 1))
                        bno = bmm(k, tl)
                        if bno >= 3:
                            t.wait_ge(soe, bno - 2)
                        last = None
                        for ht in range(HT):
                            for half in range(2):
                                last = t.matmul(
                                    psy[tl % 2][:, half * 512:(half + 1) * 512],
                                    hT[:, ht * CT + tl * P: ht * CT + (tl + 1) * P],
                                    w2[:, ht * D + half * 512: ht * D + (half + 1) * 512],
                                    start=(ht == 0), stop=(ht == HT - 1))
                        last.then_inc(sbm, 1)

            @b.scalar
            def _(s0):
                s = p.chain(s0, f"{tag}_sch")
                for k in range(NCH):
                    s0.wait_ge(scb1, k + 1)
                    for ht in range(HT):
                        s0.wait_ge(sha, amm(k, ht))
                        if ht == 0 and k > 0:
                            s0.wait_ge(sbm, k * CTT)
                        s.activation(hT[:, ht * CT:(ht + 1) * CT], psa[ht % 2][:, :],
                                     AF.Silu, bias=b1col[:, ht:ht + 1],
                                     inc=(shs, amm(k, ht)))

            @b.vector
            def _(v0):
                v = p.chain(v0, f"{tag}_vch")
                for k in range(NCH):
                    v0.wait_ge(stb1, k + 1)
                    if k > 0:
                        v0.wait_ge(shs, amm(k - 1, HT - 1))
                    v.tensor_copy(b1col[:, :], psa[0][:, 0:HT], inc=(scb1, k + 1))
                    v0.wait_ge(sb2, 16 * (k + 1))
                    if k >= 1:
                        v0.wait_ge(sha, 32 * k)
                    for n in range(HT):
                        tl, dt = divmod(n, DT)
                        v0.wait_ge(sstr, 32 * k + n + 1)
                        v.tensor_copy(lnGT[:, dt * CT + tl * P: dt * CT + (tl + 1) * P],
                                      trp2[n % 2][:, :], inc=(scg, 32 * k + n + 1))
                    for tl in range(CTT):
                        bno = bmm(k, tl)
                        v0.wait_ge(sbm, bno)
                        if bno >= 3:
                            v0.wait_ge(sod[tl % 2], 16 * ((bno - 2 + 1) // 2))
                        v.tensor_tensor(ost[tl % 2][:, :], psy[tl % 2][:, :],
                                        b2B[:, :], ALU.add, inc=(soe, bno))
                for tt in range(TT):
                    v0.wait_ge(sga[tt % 2], 16 * (tt // 2 + 1))
                    if tt >= 2:
                        v0.wait_ge(sac[tt % 2], 16 * (tt // 2))
                    v.tensor_scalar(yst[tt % 2][:, :], gst[tt % 2][:, :],
                                    p.halfw[:, tt:tt + 1], None, ALU.mult,
                                    inc=(syw, tt + 1))
    p.phase_ctx2.__exit__(None, None, None)


def emit_attention(p, io, wait_lnT):
    nc = p.nc
    p.phase_ctx3 = p.phase()
    p.phase_ctx3.__enter__()
    sqlc = [p.sem(f"at_sqlc{i}") for i in range(2)]
    swv = p.gsem("at_swv")
    sbv = p.sem("at_sbv")
    sbt = p.sem("at_sbt")
    sbc = p.sem("at_sbc")
    sall = p.sem("at_sall")
    sevk = p.sem("at_sevk")
    sevv = p.sem("at_sevv")
    sqs = p.sem("at_sqs")
    sqh = [p.sem(f"at_sqh{i}") for i in range(2)]
    sp1m = p.sem("at_sp1m")
    sp1v = p.sem("at_sp1v")
    sp1e = p.sem("at_sp1e")
    sfd = p.sem("at_sfd")
    sp1l = p.sem("at_sp1l")
    saugt = p.sem("at_saugt")
    saugc = p.sem("at_saugc")
    sbb = [p.sem(f"at_sbb{i}") for i in range(4)]
    slv = [p.sem(f"at_slv{i}") for i in range(2)]
    sp2m = p.sem("at_sp2m")
    sp2v = p.sem("at_sp2v")
    sp2e = p.sem("at_sp2e")
    sp2c = p.sem("at_sp2c")
    sctxv = p.sem("at_sctxv")
    scst = [p.sem(f"at_scst{i}") for i in range(2)]
    swo = p.gsem("at_swo")
    sbo = p.sem("at_sbo")
    scl = [p.sem(f"at_scl{i}") for i in range(2)]
    som = p.sem("at_som")
    soe = p.sem("at_soe")
    soa = [p.gsem(f"at_soa{i}") for i in range(2)]
    for _cn in ("at_qkv_sch", "at_qkv_vch", "at_core_vch", "at_core_sch", "at_op_vch"):
        p.sem(_cn)

    KTB = []
    run = 0
    for h in range(NH):
        row = []
        for qb in range(CH):
            row.append(run)
            run += 4 * qb + 4
        KTB.append(row)
    KT_TOTAL = run
    DIAG = {}
    _d = 0
    for _h in range(NH):
        for _qb in range(CH):
            for _kt in range(4 * _qb, 4 * _qb + 4):
                _d += 1
                DIAG[(_h, _qb, _kt)] = _d

    p.clear_block("at")
    if True:
        kT = p.sbuf("at_kT", [P, DT * T], BF16)
        v_sb = p.sbuf("at_v", [P, TT * D], BF16)

        sub = contextlib.ExitStack()
        old = p.pstack
        p.pstack = sub
        wq = p.sbuf("at_wq", [P, DT * D], BF16)
        wk = p.sbuf("at_wk", [P, DT * D], BF16)
        wv = p.sbuf("at_wv", [P, DT * D], BF16)
        lnc = [p.sbuf(f"at_lnc{i}", [P, DT * CT], BF16) for i in range(2)]
        qstage = p.sbuf("at_qst", [P, DT * CT], BF16)
        b8r = p.sbuf("at_b8r", [DT, 2 * P], F32)
        bqcol = p.sbuf("at_bqcol", [P, DT], F32)
        bkcol = p.sbuf("at_bkcol", [P, DT], F32)
        bvB = p.sbuf("at_bvB", [P, D], BF16)
        qsc = p.sbuf("at_qsc", [P, 1], F32)
        pk = [p.psum(f"at_pk{i}", [P, 512], F32) for i in range(2)]
        pv = [p.psum(f"at_pv{i}", [P, 512], F32) for i in range(2)]
        pbq = p.psum("at_pbq", [P, DT], F32)
        pbk = p.psum("at_pbk", [P, DT], F32)
        p.pstack = old

        with nc.Block("at_qkv") as b:
            @b.gpsimd
            def _(g):
                for w_sb, w_hbm in ((wq, io.wq), (wk, io.wk), (wv, io.wv)):
                    g.dma_start(
                        w_sb[:, :].rearrange("p (dt d) -> p dt d", dt=DT),
                        w_hbm.rearrange("(dt p) d -> p dt d", p=P)).then_inc(swv, 16)
                g.dma_start(bvB[:, :],
                            bass.AP(io.bv.tensor, 0, [[0, P], [1, D]])).then_inc(swv, 16)

            @b.sync
            def _(s):
                s.dma_start(b8r[:, 0:P],
                            io.bq.rearrange("(dt p) -> dt p", p=P)).then_inc(sbv, 16)
                s.dma_start(b8r[:, P:2 * P],
                            io.bk.rearrange("(dt p) -> dt p", p=P)).then_inc(sbv, 16)

                for c in range(CH):
                    if c >= 2:
                        s.wait_ge(sall, (c - 1) * 24)
                    s.dma_start(
                        lnc[c % 2][:, :].rearrange("p (dt t) -> p dt t", dt=DT),
                        io.lnT.rearrange("(dt p) t -> p dt t", p=P)
                        [:, :, c * CT:(c + 1) * CT],
                    ).then_inc(sqlc[c % 2], 16)
                    s.wait_ge(sevk, c * 16 + 16)
                    s.dma_start(
                        io.qT.rearrange("(dt p) t -> p dt t", p=P)
                        [:, :, c * CT:(c + 1) * CT],
                        qstage[:, :].rearrange("p (dt t) -> p dt t", dt=DT),
                    ).then_inc(sqs, 16)
                s.wait_ge(sqs, 16 * CH)

            @b.tensor
            def _(t):
                t.wait_ge(sbv, 32)
                t.transpose(pbq[:, :], b8r[:, 0:P], p.ident_f32[0:DT, 0:DT])
                t.transpose(pbk[:, :], b8r[:, P:2 * P],
                            p.ident_f32[0:DT, 0:DT]).then_inc(sbt, 1)
                t.wait_ge(swv, 64)
                for c in range(CH):
                    t.wait_ge(sqlc[c % 2], 16 * (c // 2 + 1))
                    for grp in range(16):
                        w_sb = wk if grp < 8 else wq
                        dout = grp % 8
                        gidx = c * 16 + grp
                        if gidx >= 2:
                            t.wait_ge(sevk, gidx - 1)
                        last = None
                        for din in range(DT):
                            last = t.matmul(
                                pk[grp % 2][:, :],
                                w_sb[:, din * D + dout * P: din * D + (dout + 1) * P],
                                lnc[c % 2][:, din * CT:(din + 1) * CT],
                                start=(din == 0), stop=(din == DT - 1))
                        last.then_inc(sall, 1)
                    for grp in range(8):
                        tt, half = divmod(grp, 2)
                        gidx = c * 8 + grp
                        if gidx >= 2:
                            t.wait_ge(sevv, gidx - 1)
                        last = None
                        for din in range(DT):
                            last = t.matmul(
                                pv[grp % 2][:, :],
                                lnc[c % 2][:, din * CT + tt * P: din * CT + (tt + 1) * P],
                                wv[:, din * D + half * 512: din * D + (half + 1) * 512],
                                start=(din == 0), stop=(din == DT - 1))
                        last.then_inc(sall, 1)

            @b.scalar
            def _(s0):
                s = p.chain(s0, "at_qkv_sch")
                s0.wait_ge(sbc, 1)
                for c in range(CH):
                    for grp in range(16):
                        dout = grp % 8
                        s0.wait_ge(sall, c * 24 + grp + 1)
                        if grp < 8:
                            s.activation(kT[:, dout * T + c * CT: dout * T + (c + 1) * CT],
                                         pk[grp % 2][:, :], AF.Identity,
                                         bias=bkcol[:, dout:dout + 1],
                                         inc=(sevk, c * 16 + grp + 1))
                        else:
                            if c >= 1 and grp == 8:
                                s0.wait_ge(sqs, 16 * c)
                            s.activation(qstage[:, dout * CT:(dout + 1) * CT],
                                         pk[grp % 2][:, :], AF.Identity,
                                         bias=bqcol[:, dout:dout + 1],
                                         scale=qsc[:, :],
                                         inc=(sevk, c * 16 + grp + 1))

            @b.vector
            def _(v0):
                v = p.chain(v0, "at_qkv_vch")
                v0.wait_ge(sbt, 1)
                v.tensor_copy(bqcol[:, :], pbq[:, :])
                v.tensor_scalar(bqcol[:, :], bqcol[:, :], 0.125, None, ALU.mult)
                v.tensor_copy(bkcol[:, :], pbk[:, :])
                v.memset(qsc[:, :], 0.125, inc=(sbc, 1))
                v0.wait_ge(swv, 64)
                for c in range(CH):
                    for grp in range(8):
                        tt, half = divmod(grp, 2)
                        v0.wait_ge(sall, c * 24 + 16 + grp + 1)
                        v.tensor_tensor(
                            v_sb[:, (c * CTT + tt) * D + half * 512:
                                 (c * CTT + tt) * D + (half + 1) * 512],
                            pv[grp % 2][:, :], bvB[:, half * 512:(half + 1) * 512],
                            ALU.add, inc=(sevv, c * 8 + grp + 1))
        sub.close()

        sub = contextlib.ExitStack()
        old = p.pstack
        p.pstack = sub
        qh = p.sbuf("at_qh", [P, T], BF16)
        p1scr = p.sbuf("at_p1scr", [P, 1024], BF16)
        pT = [p.sbuf(f"at_pT{i}", [P, 512], BF16) for i in range(2)]
        ctxp = [p.sbuf(f"at_ctxp{i}", [P, T], BF16) for i in range(2)]
        LINV32 = p.sbuf("at_LINV32", [P, TT], F32)
        LRUN32 = p.sbuf("at_LRUN32", [P, TT], F32)
        nc20 = p.nc20
        linvrow = p.sbuf("at_linvrow", [4, P], F32)
        linvB = [p.sbuf(f"at_linvB{i}", [HD, CT], F32) for i in range(2)]
        cs32 = p.sbuf("at_cs32", [P, 32], F32)
        psA = p.psum("at_psA", [P, T], F32)
        p.pstack = old

        p1s = [psA[:, 0:1024], psA[:, 1024:2048]]
        psT = [psA[:, 2048:2560], psA[:, 2560:3072]]
        psctx = [psA[0:HD, 3072:3584], psA[0:HD, 3584:4096]]
        pli = psA[0:4, 2048:2048 + P]

        def p1_chunks(qt):
            km = (qt + 1) * P
            return [(j, j * 1024, min(1024, km - j * 1024))
                    for j in range((km + 1023) // 1024)]

        CG = {}
        _c = 0
        for _h in range(NH):
            for _qt in range(TT):
                for (_j, _o, _w) in p1_chunks(_qt):
                    _c += 1
                    CG[(_h, _qt, _j)] = _c
        KTC = {}
        _c = 0
        for _h in range(NH):
            for _qb in range(CH):
                for _kt in range(4 * _qb + 4):
                    _c += 1
                    KTC[(_h, _qb, _kt)] = _c

        def kt_width(qb, kt):
            s = kt - 4 * qb
            return 512 - s * P if s >= 0 else 512

        with nc.Block("at_core") as b:
            @b.sync
            def _(s):
                def store_pair(pr):
                    s.wait_ge(sctxv, (2 * pr + 2) * 8)
                    s.dma_start(io.ctxT[pr * P:(pr + 1) * P, :],
                                ctxp[pr % 2][:, :]).then_inc(scst[pr % 2], 16)

                def bcast_slot(gqb):
                    s.wait_ge(saugc, gqb + 1)
                    if gqb >= 2:
                        for j in range(4):
                            s.wait_ge(sbb[j], 16 * (gqb - 1))
                    s.dma_start(io.linv_dr[(gqb % 2) * 4:(gqb % 2) * 4 + 4, :],
                                linvrow[:, :]).then_inc(slv[gqb % 2], 16)
                    s.wait_ge(slv[gqb % 2], 16 * (gqb // 2 + 1))
                    if gqb >= 2:
                        s.wait_ge(sctxv, gqb - 1)
                    for j in range(4):
                        s.dma_start(
                            linvB[gqb % 2][:, j * P:(j + 1) * P],
                            bass.AP(io.linv_dr.tensor,
                                    ((gqb % 2) * 4 + j) * P,
                                    [[0, HD], [1, P]]),
                        ).then_inc(sbb[j], 16)

                for h in range(NH):
                    if h >= 2:
                        s.wait_ge(sctxv, (h - 1) * 8)
                    s.dma_start(qh[(h % 2) * HD:(h % 2 + 1) * HD, :],
                                io.qT[h * HD:(h + 1) * HD, :]).then_inc(sqh[h % 2], 16)
                    if h >= 1:
                        for qb in range(CH):
                            bcast_slot((h - 1) * CH + qb)
                    if h >= 3 and h % 2 == 1:
                        store_pair((h - 3) // 2)
                for qb in range(CH):
                    bcast_slot((NH - 1) * CH + qb)
                store_pair(NH // 2 - 1)
                s.wait_ge(scst[0], 16 * (NH // 4))
                s.wait_ge(scst[1], 16 * (NH // 4))

            @b.tensor
            def _(t):
                def p1_mms_qt(h, qt):
                    kbase = (h % 2) * HD
                    kcol = (h // 2) * T
                    if qt == 0:
                        t.wait_ge(sqh[h % 2], 16 * (h // 2 + 1))
                    chunks = p1_chunks(qt)
                    for (j, off, w) in chunks:
                        cg = CG[(h, qt, j)]
                        dl = qt * P - off if j == len(chunks) - 1 else -1
                        if cg >= 3:
                            t.wait_ge(sp1e, cg - 2)
                        last = None
                        for j2 in range(0, w, 512):
                            w2 = min(512, w - j2)
                            has_diag = j2 <= dl < j2 + w2
                            last = t.matmul(
                                p1s[cg % 2][:, j2:j2 + w2],
                                qh[kbase:kbase + HD, qt * P:(qt + 1) * P],
                                kT[kbase:kbase + HD,
                                   kcol + off + j2: kcol + off + j2 + w2],
                                start=True, stop=not has_diag)
                            if has_diag:
                                last = t.matmul(p1s[cg % 2][:, dl:dl + P],
                                                p.ident_bf[:, :], p.mneg1[:, :],
                                                start=False, stop=True)
                        last.then_inc(sp1m, 1)

                def p2_mms_qb(h, qb):
                    kbase = (h % 2) * HD
                    kcol = (h // 2) * T
                    gqb = h * CH + qb
                    t.wait_ge(sp1l, h * TT + 4 * qb + 4)
                    t.wait_ge(sctxv, gqb)
                    if gqb >= 1:
                        t.wait_ge(saugc, gqb)
                        t.wait_ge(sp2e, KTB[h][qb])
                    t.transpose(pli, LINV32[:, 4 * qb:4 * qb + 4],
                                p.ident_f32[:, :]).then_inc(saugt, 1)
                    nkt = 4 * qb + 4

                    def ctx_mm(kt):
                        cnt = KTC[(h, qb, kt)]
                        w = kt_width(qb, kt)
                        sb_ = kt - 4 * qb
                        qoff = sb_ * P if sb_ >= 0 else 0
                        t.wait_ge(sp2e, cnt)
                        t.matmul(psctx[qb % 2][:, qoff:512],
                                 v_sb[:, kt * D + h * HD: kt * D + (h + 1) * HD],
                                 pT[kt % 2][:, 0:w],
                                 start=(kt == 0), stop=(kt == nkt - 1)
                                 ).then_inc(sp2c, 1)

                    for kt in range(nkt):
                        cnt = KTC[(h, qb, kt)]
                        w = kt_width(qb, kt)
                        sb_ = kt - 4 * qb
                        qoff = sb_ * P if sb_ >= 0 else 0
                        if kt == 0:
                            t.wait_ge(saugc, gqb + 1)
                        if cnt >= 3:
                            t.wait_ge(sp2e, cnt - 2)
                        inst = t.matmul(psT[kt % 2][:, 0:w],
                                        kT[kbase:kbase + HD,
                                           kcol + kt * P: kcol + (kt + 1) * P],
                                        qh[kbase:kbase + HD,
                                           qb * CT + qoff:(qb + 1) * CT],
                                        start=True, stop=(sb_ < 0))
                        if sb_ >= 0:
                            inst = t.matmul(psT[kt % 2][:, 0:P],
                                            p.ident_bf[:, :], p.mnegT[:, :],
                                            start=False, stop=True)
                        inst.then_inc(sp2m, 1)
                        if kt > 0:
                            ctx_mm(kt - 1)
                    ctx_mm(nkt - 1)

                for qt in range(TT):
                    p1_mms_qt(0, qt)
                for h in range(NH):
                    for qb in range(CH):
                        if h + 1 < NH:
                            for qt in range(4 * qb, 4 * qb + 4):
                                p1_mms_qt(h + 1, qt)
                        p2_mms_qb(h, qb)

            @b.vector
            def _(v0):
                v = p.chain(v0, "at_core_vch")

                def p1_v_qt(h, qt):
                    gqt = h * TT + qt
                    chunks = p1_chunks(qt)
                    v0.wait_ge(sp1l, max(gqt + 1 - TT, 0))
                    if h >= 1:
                        v0.wait_ge(saugt, (h - 1) * CH + qt // 4 + 1)
                    for (j, off, w) in chunks:
                        cg = CG[(h, qt, j)]
                        v0.wait_ge(sp1e, cg)
                        lr = LRUN32[:, qt:qt + 1]
                        if j == 0:
                            v.tensor_copy(lr, cs32[:, cg % 32:cg % 32 + 1],
                                          inc=(sfd, cg))
                        else:
                            v.tensor_tensor(lr, lr,
                                            cs32[:, cg % 32:cg % 32 + 1], ALU.add,
                                            inc=(sfd, cg))
                    v.reciprocal(LINV32[:, qt:qt + 1], LRUN32[:, qt:qt + 1],
                                 inc=(sp1l, gqt + 1))

                def p2_v_pre(h, qb):
                    gqb = h * CH + qb
                    v0.wait_ge(saugt, gqb + 1)
                    if gqb >= 1:
                        v0.wait_ge(slv[(gqb - 1) % 2], 16 * ((gqb - 1) // 2 + 1))
                    v.tensor_copy(linvrow[:, :], pli, inc=(saugc, gqb + 1))

                def p2_v_post(h, qb):
                    gqb = h * CH + qb
                    nkt = 4 * qb + 4
                    v0.wait_ge(sp2c, KTC[(h, qb, nkt - 1)])
                    for j in range(4):
                        v0.wait_ge(sbb[j], 16 * (gqb + 1))
                    if qb == 0 and h % 2 == 0 and h >= 4:
                        v0.wait_ge(scst[(h // 2) % 2], 16 * (h // 4))
                    v.tensor_tensor(ctxp[(h // 2) % 2][(h % 2) * HD:(h % 2 + 1) * HD,
                                                       qb * CT:(qb + 1) * CT],
                                    psctx[qb % 2], linvB[gqb % 2][:, :], ALU.mult,
                                    inc=(sctxv, gqb + 1))

                for qt in range(TT):
                    p1_v_qt(0, qt)
                for h in range(NH):
                    for qb in range(CH):
                        p2_v_pre(h, qb)
                        if h + 1 < NH:
                            for qt in range(4 * qb, 4 * qb + 4):
                                p1_v_qt(h + 1, qt)
                        p2_v_post(h, qb)

            @b.scalar
            def _(s0):
                s = p.chain(s0, "at_core_sch")

                def p1_s_exps(h, qt):
                    chunks = p1_chunks(qt)
                    for (j, off, w) in chunks:
                        cg = CG[(h, qt, j)]
                        s0.wait_ge(sp1m, cg)
                        if cg >= 33:
                            s0.wait_ge(sfd, cg - 32)
                        s.activation(p1scr[:, 0:w], p1s[cg % 2][:, 0:w], AF.Exp,
                                     bias=nc20[:, :],
                                     accum_out=cs32[:, cg % 32:cg % 32 + 1],
                                     inc=(sp1e, cg))

                def p1_s_qt(h, qt):
                    p1_s_exps(h, qt)

                def p2_s_qb(h, qb):
                    nkt = 4 * qb + 4
                    for kt in range(nkt):
                        cnt = KTC[(h, qb, kt)]
                        w = kt_width(qb, kt)
                        s0.wait_ge(sp2m, cnt)
                        if cnt >= 3:
                            s0.wait_ge(sp2c, cnt - 2)
                        s.activation(pT[kt % 2][:, 0:w], psT[kt % 2][:, 0:w],
                                     AF.Exp, bias=nc20[:, :], inc=(sp2e, cnt))

                for qt in range(TT):
                    p1_s_qt(0, qt)
                for h in range(NH):
                    for qb in range(CH):
                        if h + 1 < NH:
                            for qt in range(4 * qb, 4 * qb + 4):
                                p1_s_exps(h + 1, qt)
                        p2_s_qb(h, qb)

        sub.close()
        sub = contextlib.ExitStack()
        old = p.pstack
        p.pstack = sub
        wo = p.sbuf("at_wo", [P, DT * D], BF16)
        boB = p.sbuf("at_boB", [P, D], F32)
        cstage = [p.sbuf(f"at_cst{i}", [P, DT * P], BF16) for i in range(2)]
        yo = [p.sbuf(f"at_yo{i}", [P, D], F32) for i in range(2)]
        po = [p.psum(f"at_po{i}", [P, D], F32) for i in range(2)]
        p.pstack = old

        with nc.Block("at_outproj") as b:
            @b.gpsimd
            def _(g):
                g.dma_start(wo[:, :].rearrange("p (dt d) -> p dt d", dt=DT),
                            io.wo.rearrange("(dt p) d -> p dt d", p=P)).then_inc(swo, 16)
                for tt in range(TT):
                    g.wait_ge(soe, tt + 1)
                    g.dma_start(io.xwork[tt * P:(tt + 1) * P, :], yo[tt % 2][:, :],
                                accum_op=ALU.add).then_inc(soa[tt % 2], 16)
                g.wait_ge(soa[0], 16 * (TT // 2))
                g.wait_ge(soa[1], 16 * (TT // 2))

            @b.sync
            def _(s):
                s.dma_start(boB[:, :],
                            bass.AP(io.bo.tensor, 0, [[0, P], [1, D]])).then_inc(sbo, 16)
                for tt in range(TT):
                    if tt >= 2:
                        s.wait_ge(som, tt - 1)
                    s.dma_start(
                        cstage[tt % 2][:, :].rearrange("p (pr t) -> p pr t", pr=DT),
                        io.ctxT.rearrange("(pr p) t -> p pr t", p=P)
                        [:, :, tt * P:(tt + 1) * P],
                    ).then_inc(scl[tt % 2], 16)

            @b.tensor
            def _(t):
                t.wait_ge(swo, 16)
                for tt in range(TT):
                    t.wait_ge(scl[tt % 2], 16 * (tt // 2 + 1))
                    if tt >= 2:
                        t.wait_ge(soe, tt - 1)
                    last = None
                    for dm in range(DT):
                        for half in range(2):
                            last = t.matmul(
                                po[tt % 2][:, half * 512:(half + 1) * 512],
                                cstage[tt % 2][:, dm * P:(dm + 1) * P],
                                wo[:, dm * D + half * 512: dm * D + (half + 1) * 512],
                                start=(dm == 0), stop=(dm == DT - 1))
                    last.then_inc(som, tt + 1) if False else last.then_inc(som, 1)

            @b.vector
            def _(v0):
                v = p.chain(v0, "at_op_vch")
                v0.wait_ge(sbo, 16)
                for tt in range(TT):
                    v0.wait_ge(som, tt + 1)
                    if tt >= 2:
                        v0.wait_ge(soa[tt % 2], 16 * (tt // 2))
                    v.tensor_tensor(yo[tt % 2][:, :], po[tt % 2][:, :], boB[:, :],
                                    ALU.add, inc=(soe, tt + 1))
        sub.close()
    p.phase_ctx3.__exit__(None, None, None)


class IO:
    pass


def build_io(nc):
    io = IO()
    io.x_in = nc.dram_tensor("x", [T, D], F32, kind="ExternalInput").ap()
    io.wg = nc.dram_tensor("Wg", [D, E], F32, kind="ExternalInput").ap()
    io.bg = nc.dram_tensor("bg", [E], F32, kind="ExternalInput").ap()
    io.w1 = nc.dram_tensor("W1", [E, D, HID], F32, kind="ExternalInput").ap()
    io.b1 = nc.dram_tensor("b1", [E * HT, P], F32, kind="ExternalInput").ap()
    io.w2 = nc.dram_tensor("W2", [E, HID, D], F32, kind="ExternalInput").ap()
    io.b2 = nc.dram_tensor("b2", [E, D], F32, kind="ExternalInput").ap()
    io.wq = nc.dram_tensor("Wq", [D, D], F32, kind="ExternalInput").ap()
    io.bq = nc.dram_tensor("bq", [D], F32, kind="ExternalInput").ap()
    io.wk = nc.dram_tensor("Wk", [D, D], F32, kind="ExternalInput").ap()
    io.bk = nc.dram_tensor("bk", [D], F32, kind="ExternalInput").ap()
    io.wv = nc.dram_tensor("Wv", [D, D], F32, kind="ExternalInput").ap()
    io.bv = nc.dram_tensor("bv", [D], F32, kind="ExternalInput").ap()
    io.wo = nc.dram_tensor("Wo", [D, D], F32, kind="ExternalInput").ap()
    io.bo = nc.dram_tensor("bo", [D], F32, kind="ExternalInput").ap()
    io.y = nc.dram_tensor("y", [T, D], F32, kind="ExternalOutput").ap()
    io.xwork = nc.dram_tensor("xwork", [T, D], F32, kind="Internal").ap()
    io.o_sp = nc.dram_tensor("o_sp", [NCH * CT, D], F32, kind="Internal").ap()
    io.lnR = nc.dram_tensor("lnR", [T, D], BF16, kind="Internal").ap()
    io.tokof = nc.dram_tensor("tokof", [NCH * CT, 1], I32, kind="Internal").ap()
    io.W1bf = nc.dram_tensor("W1bf", [E * P, DT * HID], BF16, kind="Internal").ap()
    io.W2bf = nc.dram_tensor("W2bf", [E * P, HT * D], BF16, kind="Internal").ap()
    io.lnT = nc.dram_tensor("lnT", [D, T], BF16, kind="Internal").ap()
    io.qT = nc.dram_tensor("qTs", [D, T], BF16, kind="Internal").ap()
    io.linv_dr = nc.dram_tensor("linv_dr", [8, P], F32, kind="Internal").ap()
    io.ctxT = nc.dram_tensor("ctxTs", [D, T], BF16, kind="Internal").ap()
    return io


ALL_PHASES = ("setup", "ln1", "ffn1", "ln2", "qkv", "attn", "outproj", "ln3",
              "copyout", "ffn2")


def build_program(phases=ALL_PHASES, debug_out=()):
    nc = bass.Bass("TRN2", target_bir_lowering=False)
    io = build_io(nc)

    with contextlib.ExitStack() as stack:
        p = Prog(nc, stack)
        for _gn in ("setup_dma_g", "at_swv", "at_soa0", "at_soa1", "at_swo",
                    "ffn1_sw1", "ffn1_sw2", "ffn1_sb2", "ffn1_sac0", "ffn1_sac1",
                    "ffn2_sw1", "ffn2_sw2", "ffn2_sb2", "ffn2_sac0", "ffn2_sac1",
                    "ffn1_sga0", "ffn1_sga1", "ffn2_sga0", "ffn2_sga1",
                    "ffn1_sbr", "ffn2_sbr",
                    "ffn1_slr0", "ffn1_slr1", "ffn1_slr2", "ffn1_slr3",
                    "ffn2_slr0", "ffn2_slr1", "ffn2_slr2", "ffn2_slr3",
                    "ln1_slg0", "ln1_slg1", "ln3_slg0", "ln3_slg1",
                    "ln1_swc0", "ln1_swc1"):
            p.gsem(_gn)
        finals = []
        if "setup" in phases:
            emit_setup(p, io)
        if "ln1" in phases:
            emit_ln(p, io, 1, io.x_in, with_gate=True, precast=True)
        if "ffn1" in phases:
            emit_ffn_sparse(p, io, 1, io.xwork)
        if "ln2" in phases:
            emit_ln(p, io, 2, io.xwork, with_gate=False)
        if "attn" in phases:
            emit_attention(p, io, wait_lnT=None)
        if "ln3" in phases:
            emit_ln(p, io, 3, io.xwork, with_gate=True)
        if "copyout" in phases:
            with p.phase():
                sco = p.sem("s_copyout")
                p.clear_block("copyout")
                with nc.Block("copyout") as b:
                    @b.sync
                    def _(s):
                        s.dma_start(io.y[:, :], io.xwork[:, :]).then_inc(sco, 16)
                        s.wait_ge(sco, 16)
        if "ffn2" in phases:
            emit_ffn_sparse(p, io, 2, io.y)

        if debug_out:
            with p.phase():
                sdbg = p.sem("s_dbg")
                p.clear_block("dbg")
                with nc.Block("dbg") as b:
                    @b.sync
                    def _(s):
                        m = 0
                        for name in debug_out:
                            srcap = getattr(io, name)
                            dbg = nc.dram_tensor(f"dbg_{name}", list(srcap.shape),
                                                 srcap.tensor.dtype,
                                                 kind="ExternalOutput").ap()
                            s.dma_start(dbg, srcap).then_inc(sdbg, 16)
                            m += 16
                        s.wait_ge(sdbg, m)

        with nc.Block("final") as b:
            @b.sync
            def _(s):
                s.nop()
    return nc


def make_in_map(inputs, core):
    return dict(
        x=np.ascontiguousarray(inputs["L0_attn_x"][core]),
        Wg=inputs["L0_ffn_Wg"], bg=inputs["L0_ffn_bg"],
        W1=inputs["L0_ffn_W1"], b1=inputs["L0_ffn_b1"],
        W2=inputs["L0_ffn_W2"], b2=inputs["L0_ffn_b2"],
        Wq=inputs["L0_attn_Wq"], bq=inputs["L0_attn_bq"],
        Wk=inputs["L0_attn_Wk"], bk=inputs["L0_attn_bk"],
        Wv=inputs["L0_attn_Wv"], bv=inputs["L0_attn_bv"],
        Wo=inputs["L0_attn_Wo"], bo=inputs["L0_attn_bo"],
    )


def kernel(**inputs):
    inputs = {k: np.asarray(v, dtype=np.float32) for k, v in inputs.items()}
    nc = build_program(ALL_PHASES)
    in_maps = [make_in_map(inputs, c) for c in range(8)]
    res = run_bass_kernel_spmd(nc, in_maps, core_ids=list(range(8)))
    return np.stack([res.results[c]["y"] for c in range(8)], axis=0)
